# version 05
# baseline (speedup 1.0000x reference)
import numpy as np
import ml_dtypes
import concourse.bass as bass
import concourse.mybir as mybir
from concourse.bass_utils import run_bass_kernel_spmd

F32 = mybir.dt.float32
BF16 = mybir.dt.bfloat16
AF = mybir.ActivationFunctionType
ALU = mybir.AluOpType
AX = mybir.AxisListType
BF = ml_dtypes.bfloat16

EPS = 1e-5
DFF = 5504
FT = 43
KT = 16
NT = 32
NBLK = 8
TT = 8

DEBUG_TAPS = False


class Prog:

    def __init__(self, nc, sem_names):
        self.nc = nc
        self.q = {k: [] for k in "TVSGY"}
        self.c = {n: 0 for n in sem_names}
        self.sems = {}
        self._cms = []
        for n in sem_names:
            cm = nc.semaphore(n)
            self._cms.append(cm)
            self.sems[n] = cm.__enter__()

    def op(self, e, fn, inc=None, n=1):
        if inc is None:
            self.q[e].append(fn)
        else:
            self.c[inc] += n
            s = self.sems[inc]
            self.q[e].append(lambda eng, fn=fn, s=s, n=n: fn(eng).then_inc(s, n))

    def wait(self, e, sem, val=None):
        v = self.c[sem] if val is None else val
        assert 0 <= v <= self.c[sem], f"wait on future value of {sem}: {v} > {self.c[sem]}"
        if v == 0:
            return
        s = self.sems[sem]
        self.q[e].append(lambda eng, s=s, v=v: eng.wait_ge(s, v))

    def lvl(self, sem):
        return self.c[sem]


def build_program():
    nc = bass.Bass("TRN2", target_bir_lowering=False)
    dram = lambda name, shape, dt, kind: nc.dram_tensor(name, shape, dt, kind=kind).ap()
    EI, EO, IN = "ExternalInput", "ExternalOutput", "Internal"

    xTh = dram("xTh", [NT, 128, 2048], F32, EI)
    wqk = dram("wqk", [128, KT * 1024], F32, EI)
    wv = dram("wv", [128, KT * 512], F32, EI)
    ropeC = dram("ropeC", [128, NT * 64], F32, EI)
    ropeS = dram("ropeS", [128, NT * 64], F32, EI)
    lncol = dram("lncol", [128, 2], F32, EI)
    ebrow = dram("ebrow", [1, 8], F32, EI)
    maskst = dram("maskst", [128, 896], F32, EI)
    eyebf = dram("eyebf", [128, 128], BF16, EI)
    xres = dram("xres", [TT, 128, 2048], F32, EI)
    woth = dram("woth", [128, 4 * 2048], F32, EI)
    wsm = dram("wsm", [128, KT * 152], F32, EI)
    wgt_h = dram("wgt_h", [FT, 128, 2048], F32, EI)
    wup_h = dram("wup_h", [FT, 128, 2048], F32, EI)
    wdn_h = dram("wdn_h", [FT, 128, 2048], F32, EI)
    bgd = dram("bgd", [72, DFF], F32, EI)
    bud = dram("bud", [72, DFF], F32, EI)
    bdd = dram("bdd", [72, 2048], F32, EI)
    adn = dram("adn", [128, FT * 72], F32, EI)
    avdr = dram("avdr", [4, 128, 4096], BF16, IN)
    partial = dram("partial", [NT, 128, 2048], F32, IN)
    rsout = dram("rsout", [TT * 128 * 2048], F32, IN)
    x1scr = dram("x1scr", [TT, 128, 2048], F32, IN)
    dscr = dram("dscr", [TT, 128, 2048], F32, IN)
    out = dram("out", [TT, 128, 2048], F32, EO)
    if DEBUG_TAPS:
        dbg_qk = dram("dbg_qk", [128, 8 * 4096], BF16, EO)
        dbg_v = dram("dbg_v", [128, 4 * 4096], BF16, EO)
        dbg_av = dram("dbg_av", [4, 128, 4096], BF16, EO)
        dbg_x1 = dram("dbg_x1", [TT, 128, 2048], F32, EO)
        dbg_wgt = dram("dbg_wgt", [128, 64], F32, EO)
        dbg_hid = dram("dbg_hid", [128, FT * 1024], BF16, EO)
        dbg_ds = dram("dbg_ds", [TT, 128, 2048], F32, EO)

    from contextlib import ExitStack
    ctx = ExitStack()
    sbt = lambda name, shape, dt: ctx.enter_context(nc.sbuf_tensor(name, shape, dt)).ap()
    psn = lambda name: ctx.enter_context(nc.psum_tensor(name, [128, 512], F32)).ap()

    o_wqk = 0
    o_wv = 0
    o_xb = 16384
    o_qkT = 20480
    o_v = 53248
    o_p = 69632
    o_avs = 70656
    o_qtm = 71680
    o_wo = 0
    o_ava = 8192
    o_wg = 0
    o_wu = 4096
    o_wd = 0
    o_wsm = 8192
    o_xgT = 10624
    o_xuT = 11648
    o_xdT = 12672
    o_x1stg = 13696
    o_xastg = 15744
    o_bgb = 16000
    o_bd = 16512
    o_adn = 18560
    o_x1T = 21760
    o_hid = 38144
    AB = 82176
    ab = sbt("ab", [128, AB], BF16)
    fa = sbt("fa", [128, 6944], F32)

    c_lncol = sbt("c_lncol", [128, 2], F32)
    c_eb = sbt("c_eb", [128, 8], F32)
    c_eye = sbt("c_eye", [128, 128], BF16)
    c_1bf = sbt("c_1bf", [128, 1], BF16)
    c_1row = sbt("c_1row", [1, 128], F32)
    c_eps = sbt("c_eps", [128, 1], F32)
    c_8th = sbt("c_8th", [128, 1], F32)
    eb_st = sbt("eb_st", [1, 8], F32)

    f_rq = 0
    f_t2 = 1024
    f_bst = 1536
    f_mv = 1600
    f_sd = 1632
    f_rr = 1648
    f_cos = 1664
    f_sin = 3712
    f_tsq = 0
    f_den = 1024
    f_avf = 2048
    f_msk = 2560
    f_prt = 0
    f_rs = 0
    f_xr = 2048
    f_x1 = 4096
    f_x1d = 0
    f_stg = 1024
    f_df = 2048
    f_os = 4096
    f_ln = 6144
    f_rt = 6272
    f_xa = 6720

    pb = [psn(f"pb{i}") for i in range(8)]
    pbf = [b.bitcast(BF16) for b in pb]

    sems = ["sx", "sw", "sq", "st", "sv", "ss", "gv", "sy", "scc", "so",
            "swg", "swd", "sb2"]
    p = Prog(nc, sems)

    p.op("G", lambda g: g.dma_start(out=fa[:, f_cos:f_cos + 2048], in_=ropeC), inc="sw", n=16)
    p.op("G", lambda g: g.dma_start(out=fa[:, f_sin:f_sin + 2048], in_=ropeS), inc="sw", n=16)
    p.op("G", lambda g: g.dma_start(out=c_eye[:], in_=eyebf), inc="sw", n=16)
    p.op("G", lambda g: g.dma_start(out=c_lncol[:], in_=lncol), inc="sw", n=16)
    p.op("G", lambda g: g.dma_start(out=eb_st[:], in_=ebrow), inc="sw", n=16)
    p.op("G", lambda g: g.dma_start(out=ab[:, o_wqk:o_wqk + 16384], in_=wqk), inc="sw", n=16)
    lv_const_w = p.lvl("sw")
    p.op("V", lambda v: v.memset(c_1bf[:], 1.0))
    p.op("V", lambda v: v.memset(c_1row[:], 1.0))
    p.op("V", lambda v: v.memset(c_eps[:], EPS))
    p.op("V", lambda v: v.memset(c_8th[:], 0.125), inc="sv")
    lv_const_v = p.lvl("sv")
    p.wait("T", "sw", lv_const_w)
    p.wait("T", "sv", lv_const_v)
    p.op("T", lambda t: t.matmul(pb[6][:, 0:8], c_1row[:], eb_st[0:1, :], start=True, stop=True),
         inc="sq")
    lv_bc = p.lvl("sq")
    p.wait("S", "sq", lv_bc)
    p.op("S", lambda s: s.copy(c_eb[:], pb[6][:, 0:8]), inc="ss")
    lv_const_s = p.lvl("ss")

    lv_sq1, lv_svr, lv_st1, lv_ss1 = {}, {}, {}, {}

    def xslot(i):
        o = o_xb + (i % 2) * 2048
        return ab[:, o:o + 2048]

    for t in range(NT):
        if t >= 2:
            p.wait("G", "sq", lv_sq1[t - 2])
        p.op("G", lambda y, t=t: y.dma_start(out=xslot(t), in_=xTh[t]), inc="sx", n=16)
        lv_x = p.lvl("sx")

        p.wait("T", "sx", lv_x)
        if t >= 2:
            p.wait("T", "sv", lv_svr[t - 2])
        bq, bk = (pb[0], pb[1]) if t % 2 == 0 else (pb[2], pb[3])
        for cb, bank in ((0, bq), (1, bk)):
            for kt in range(KT):
                p.op("T", lambda t_, t=t, kt=kt, cb=cb, bank=bank: t_.matmul(
                    bank[:, 0:512],
                    xslot(t)[:, kt * 128:kt * 128 + 128],
                    ab[:, o_wqk + kt * 1024 + cb * 512: o_wqk + kt * 1024 + cb * 512 + 512],
                    start=(kt == 0), stop=(kt == KT - 1)),
                    inc=("sq" if kt == KT - 1 and cb == 1 else None))
        lv_sq1[t] = p.lvl("sq")

        p.wait("V", "sq", lv_sq1[t])
        if t == 0:
            p.wait("V", "sw", lv_const_w)
        cosT = fa[:, f_cos + t * 64:f_cos + t * 64 + 64]
        sinT = fa[:, f_sin + t * 64:f_sin + t * 64 + 64]

        def _ups(u, bq=bq, bk=bk):
            bank = bq if u % 2 == 0 else bk
            h = u // 2
            return bank[:, h * 128:h * 128 + 128]

        psl = [_ups(u) for u in range(8)]
        rq = lambda u, a, b: fa[:, f_rq + u * 128 + a: f_rq + u * 128 + b]
        t2 = lambda u: fa[:, f_t2 + u * 64: f_t2 + u * 64 + 64]
        if t >= 1:
            p.wait("V", "st", lv_st1[t - 1])
        for u in range(8):
            p.op("V", lambda v, u=u, ps_=psl[u], cosT=cosT: v.tensor_tensor(rq(u, 0, 64), ps_[:, 0:64], cosT, op=ALU.mult))
        for u in range(8):
            p.op("V", lambda v, u=u, ps_=psl[u], sinT=sinT: v.tensor_tensor(t2(u), ps_[:, 64:128], sinT, op=ALU.mult))
        for u in range(8):
            p.op("V", lambda v, u=u: v.tensor_tensor(rq(u, 0, 64), rq(u, 0, 64), t2(u), op=ALU.subtract))
        for u in range(8):
            p.op("V", lambda v, u=u, ps_=psl[u], sinT=sinT: v.tensor_tensor(t2(u), ps_[:, 0:64], sinT, op=ALU.mult))
        for u in range(8):
            p.op("V", lambda v, u=u, ps_=psl[u], cosT=cosT: v.tensor_tensor(rq(u, 64, 128), ps_[:, 64:128], cosT, op=ALU.mult))
        for u in range(8):
            p.op("V", lambda v, u=u: v.tensor_tensor(rq(u, 64, 128), t2(u), rq(u, 64, 128), op=ALU.add))
        for u in range(8):
            p.op("V", lambda v, u=u: v.bn_stats(fa[:, f_bst + u * 6:f_bst + u * 6 + 6], rq(u, 0, 128)))
        for u in range(8):
            p.op("V", lambda v, u=u: v.bn_aggr(fa[:, f_mv + u * 2:f_mv + u * 2 + 2],
                                               fa[:, f_bst + u * 6:f_bst + u * 6 + 6]),
                 inc=("sv" if u == 7 else None))
        lv_stats = p.lvl("sv")
        p.wait("S", "sv", lv_stats)
        p.op("S", lambda s: s.activation(fa[:, f_sd:f_sd + 8], fa[:, f_mv + 1:f_mv + 16:2],
                                         AF.Sqrt, bias=c_eps[:, 0:1]), inc="ss")
        lv_sd = p.lvl("ss")
        p.wait("V", "ss", lv_sd)
        p.op("V", lambda v: v.reciprocal(fa[:, f_rr:f_rr + 8], fa[:, f_sd:f_sd + 8]), inc="gv")
        p.wait("V", "gv")
        for u in range(8):
            p.op("V", lambda v, u=u: v.tensor_scalar(
                ab[:, o_qtm + u * 128:o_qtm + u * 128 + 128],
                rq(u, 0, 128),
                fa[:, f_mv + u * 2:f_mv + u * 2 + 1], fa[:, f_rr + u:f_rr + u + 1],
                op0=ALU.subtract, op1=ALU.mult),
                inc=("sv" if u == 7 else None))
        lv_svr[t] = p.lvl("sv")

        p.wait("T", "sv", lv_svr[t])
        if t >= 1:
            p.wait("T", "ss", lv_ss1[t - 1])
        for u in range(8):
            bank = pbf[4] if u < 4 else pbf[5]
            p.op("T", lambda t_, u=u, bank=bank: t_.transpose(
                bank[:, (u % 4) * 128:(u % 4) * 128 + 128],
                ab[:, o_qtm + u * 128:o_qtm + u * 128 + 128], c_eye[:]),
                inc=("st" if u == 7 else None))
        lv_st1[t] = p.lvl("st")
        p.wait("S", "st", lv_st1[t])
        if t == 0:
            p.wait("S", "sw", lv_const_w)
        for u in range(8):
            bank = pbf[4] if u < 4 else pbf[5]
            p.op("S", lambda s, u=u, t=t, bank=bank: s.activation(
                ab[:, o_qkT + u * 4096 + t * 128: o_qkT + u * 4096 + t * 128 + 128],
                bank[:, (u % 4) * 128:(u % 4) * 128 + 128],
                AF.Identity, bias=c_lncol[:, 1:2], scale=c_lncol[:, 0:1]),
                inc=("ss" if u == 7 else None))
        lv_ss1[t] = p.lvl("ss")

    p.wait("G", "sq", lv_sq1[NT - 1])
    p.op("G", lambda g: g.dma_start(out=ab[:, o_wv:o_wv + 8192], in_=wv), inc="sw", n=16)
    lv_wv = p.lvl("sw")
    p.wait("G", "sv", lv_svr[NT - 1])
    p.op("G", lambda g: g.dma_start(out=fa[:, f_msk:f_msk + 896], in_=maskst), inc="sw", n=16)
    lv_msk = p.lvl("sw")
    lv_sqv, lv_ssv = {}, {}
    for t in range(NT):
        p.wait("G", "sq", lv_sqv[t - 2] if t >= 2 else lv_sq1[NT - 1])
        p.op("G", lambda y, t=t: y.dma_start(out=xslot(t), in_=xTh[t]), inc="sx", n=16)
        lv_x = p.lvl("sx")
        p.wait("T", "sx", lv_x)
        if t == 0:
            p.wait("T", "sw", lv_wv)
            p.wait("T", "ss", lv_const_s)
        bankv = pb[6] if t % 2 == 0 else pb[7]
        if t >= 2:
            p.wait("T", "ss", lv_ssv[t - 2])
        for kt in range(KT):
            p.op("T", lambda t_, t=t, kt=kt, bankv=bankv: t_.matmul(
                bankv[:, 0:512],
                xslot(t)[:, kt * 128:kt * 128 + 128],
                ab[:, o_wv + kt * 512:o_wv + kt * 512 + 512],
                start=(kt == 0), stop=(kt == KT - 1)),
                inc=("sq" if kt == KT - 1 else None))
        lv_sqv[t] = p.lvl("sq")
        p.wait("S", "sq", lv_sqv[t])
        for h in range(4):
            p.op("S", lambda s, t=t, h=h, bankv=bankv: s.copy(
                ab[:, o_v + h * 4096 + t * 128: o_v + h * 4096 + t * 128 + 128],
                bankv[:, h * 128:h * 128 + 128]),
                inc=("ss" if h == 3 else None))
        lv_ssv[t] = p.lvl("ss")

    if DEBUG_TAPS:
        p.wait("Y", "ss", lv_ss1[NT - 1])
        p.wait("Y", "ss", lv_ssv[NT - 1])
        p.op("Y", lambda y: y.dma_start(out=dbg_qk, in_=ab[:, o_qkT:o_qkT + 8 * 4096]), inc="so", n=16)
        p.op("Y", lambda y: y.dma_start(out=dbg_v, in_=ab[:, o_v:o_v + 4 * 4096]), inc="so", n=16)

    p.wait("T", "ss", lv_ss1[NT - 1])
    p.wait("T", "ss", lv_ssv[NT - 1])
    p.wait("V", "sw", lv_msk)
    qof = lambda h: o_qkT + (2 * h) * 4096
    kof = lambda h: o_qkT + (2 * h + 1) * 4096

    g_ssq = {}
    g_p = {}
    g_av = {}
    gj = 0
    lv_dcp = 0
    lv_avbf_prev = 0
    lv_av_dma = 0
    for h in range(4):
        for i in range(NBLK):
            b = h * NBLK + i
            mtiles = 4 * i + 4
            for jj in range(mtiles):
                bs = pb[gj % 2]
                if gj >= 2:
                    p.wait("T", "ss", g_ssq[gj - 2])
                if jj == 0 and b > 0:
                    p.wait("T", "ss", lv_dcp)
                p.op("T", lambda t_, h=h, i=i, jj=jj, bs=bs: t_.matmul(
                    bs[:, 0:512],
                    ab[:, kof(h) + jj * 128: kof(h) + jj * 128 + 128],
                    ab[:, qof(h) + i * 512: qof(h) + i * 512 + 512],
                    start=True, stop=True), inc="sq")
                lv_sc = p.lvl("sq")
                p.wait("S", "sq", lv_sc)
                if gj >= 2:
                    p.wait("S", "sv", g_p[gj - 2])
                p.op("S", lambda s, gj=gj, bs=bs: s.activation(
                    fa[:, f_tsq + (gj % 2) * 512: f_tsq + (gj % 2) * 512 + 512],
                    bs[:, 0:512], AF.Square), inc="ss")
                g_ssq[gj] = p.lvl("ss")
                p.wait("V", "ss", g_ssq[gj])
                if gj >= 2:
                    p.wait("V", "sq", g_av[gj - 2])
                tsqa = fa[:, f_tsq + (gj % 2) * 512: f_tsq + (gj % 2) * 512 + 512]
                pdst = ab[:, o_p + (gj % 2) * 512: o_p + (gj % 2) * 512 + 512]
                if jj >= 4 * i:
                    kd = jj - 4 * i
                    msl = fa[:, f_msk + 384 - 128 * kd: f_msk + 384 - 128 * kd + 512]
                    p.op("V", lambda v, tsqa=tsqa: v.tensor_tensor(tsqa, tsqa, tsqa, op=ALU.mult),
                         inc="gv")
                    p.wait("V", "gv")
                    p.op("V", lambda v, tsqa=tsqa, pdst=pdst, msl=msl: v.tensor_tensor(
                        pdst, tsqa, msl, op=ALU.mult), inc="sv")
                else:
                    p.op("V", lambda v, tsqa=tsqa, pdst=pdst: v.tensor_tensor(
                        pdst, tsqa, tsqa, op=ALU.mult), inc="sv")
                g_p[gj] = p.lvl("sv")
                p.wait("T", "sv", g_p[gj])
                p.op("T", lambda t_, h=h, jj=jj, pdst=pdst, st_=(jj == 0), sp=(jj == mtiles - 1):
                     t_.matmul(pb[2][:, 0:512],
                               ab[:, o_v + h * 4096 + jj * 128: o_v + h * 4096 + jj * 128 + 128],
                               pdst, start=st_, stop=sp, skip_group_check=True))
                p.op("T", lambda t_, pdst=pdst, st_=(jj == 0), sp=(jj == mtiles - 1):
                     t_.matmul(pb[3][0:1, 0:512], c_1bf[:], pdst,
                               start=st_, stop=sp, skip_group_check=True), inc="sq")
                g_av[gj] = p.lvl("sq")
                gj += 1
            p.wait("S", "sq", g_av[gj - 1])
            p.op("S", lambda s: s.copy(fa[0:1, f_den:f_den + 512], pb[3][0:1, 0:512]))
            p.op("S", lambda s: s.copy(fa[:, f_avf:f_avf + 512], pb[2][:, 0:512]), inc="ss")
            lv_dcp = p.lvl("ss")
            p.wait("V", "ss", lv_dcp)
            p.op("V", lambda v: v.reciprocal(fa[0:1, f_den + 512:f_den + 1024],
                                             fa[0:1, f_den:f_den + 512]), inc="sv")
            lv_rec = p.lvl("sv")
            p.wait("T", "sv", lv_rec)
            if b > 0:
                p.wait("T", "sv", lv_avbf_prev)
            p.op("T", lambda t_: t_.matmul(pb[4][:, 0:512], c_1row[:],
                                           fa[0:1, f_den + 512:f_den + 1024],
                                           start=True, stop=True), inc="sq")
            lv_bcm = p.lvl("sq")
            p.wait("V", "sq", lv_bcm)
            if b >= 2:
                p.wait("V", "sy", 16 * (b - 1))
            p.op("V", lambda v, b=b: v.tensor_tensor(
                ab[:, o_avs + (b % 2) * 512: o_avs + (b % 2) * 512 + 512],
                fa[:, f_avf:f_avf + 512], pb[4][:, 0:512], op=ALU.mult), inc="sv")
            lv_avbf_prev = p.lvl("sv")
            p.wait("Y", "sv", lv_avbf_prev)
            p.op("Y", lambda y, h=h, i=i, b=b: y.dma_start(
                out=avdr[h][:, i * 512:i * 512 + 512],
                in_=ab[:, o_avs + (b % 2) * 512: o_avs + (b % 2) * 512 + 512]),
                inc="sy", n=16)
            lv_av_dma = p.lvl("sy")

    p.wait("G", "sq", lv_sqv[NT - 1])
    p.op("G", lambda g: g.dma_start(out=ab[:, o_wo:o_wo + 8192], in_=woth), inc="sw", n=16)
    lv_wo = p.lvl("sw")
    p.wait("Y", "sy", lv_av_dma)
    for h in range(4):
        p.op("Y", lambda y, h=h: y.dma_start(
            out=ab[:, o_ava + h * 4096: o_ava + h * 4096 + 4096], in_=avdr[h]), inc="sx", n=16)
    lv_ava = p.lvl("sx")
    if DEBUG_TAPS:
        p.op("Y", lambda y: y.dma_start(out=dbg_av, in_=avdr), inc="so", n=16)

    p.wait("T", "sw", lv_wo)
    p.wait("T", "sx", lv_ava)
    p.wait("T", "ss", lv_dcp)
    lv_ev = {}
    lv_pyy = {}
    k = 0
    for tt in range(NT):
        for db in range(4):
            bank = pb[k % 4]
            if k >= 4:
                p.wait("T", "sv" if (k - 4) % 2 == 0 else "ss", lv_ev[k - 4])
            for h in range(4):
                p.op("T", lambda t_, tt=tt, db=db, h=h, bank=bank: t_.matmul(
                    bank[:, 0:512],
                    ab[:, o_ava + h * 4096 + tt * 128: o_ava + h * 4096 + tt * 128 + 128],
                    ab[:, o_wo + h * 2048 + db * 512: o_wo + h * 2048 + db * 512 + 512],
                    start=(h == 0), stop=(h == 3)),
                    inc=("sq" if h == 3 else None))
            lv_mm = p.lvl("sq")
            eng = "V" if k % 2 == 0 else "S"
            p.wait(eng, "sq", lv_mm)
            if k >= 2:
                p.wait(eng, "sy", lv_pyy[k - 2])
            dst = fa[:, f_prt + (k % 2) * 512: f_prt + (k % 2) * 512 + 512]
            if eng == "V":
                p.op("V", lambda v, dst=dst, bank=bank: v.tensor_copy(dst, bank[:, 0:512]), inc="sv")
                lv_ev[k] = p.lvl("sv")
            else:
                p.op("S", lambda s, dst=dst, bank=bank: s.copy(dst, bank[:, 0:512]), inc="ss")
                lv_ev[k] = p.lvl("ss")
            p.wait("Y", "sv" if k % 2 == 0 else "ss", lv_ev[k])
            p.op("Y", lambda y, tt=tt, db=db, dst=dst: y.dma_start(
                out=partial[tt][:, db * 512:db * 512 + 512], in_=dst), inc="sy", n=16)
            lv_pyy[k] = p.lvl("sy")
            k += 1
    lv_prt_all = p.lvl("sy")

    p.wait("G", "sy", lv_prt_all)
    p.op("G", lambda g: g.collective_compute(
        "ReduceScatter", ALU.add,
        replica_groups=[[0, 1, 2, 3], [4, 5, 6, 7]],
        ins=[partial.rearrange("t p f -> (t p f)")],
        outs=[rsout]), inc="scc")
    p.op("G", lambda g: g.dma_start(out=ab[:, o_wsm:o_wsm + 2432], in_=wsm), inc="sw", n=16)
    p.op("G", lambda g: g.dma_start(out=ab[0:72, o_bd:o_bd + 2048], in_=bdd), inc="sw", n=16)
    p.op("G", lambda g: g.dma_start(out=ab[:, o_adn:o_adn + 3096], in_=adn), inc="sw", n=16)
    lv_p4w = p.lvl("sw")

    rsv = rsout.rearrange("(t p f) -> t p f", t=TT, p=128)
    for e in "TVSY":
        p.wait(e, "scc", 1)
    lv_x1y = {}
    lv_cast = {}
    lv_x1T_s = 0
    for tt in range(TT):
        p.op("Y", lambda y, tt=tt: y.dma_start(out=fa[:, f_rs:f_rs + 2048], in_=rsv[tt]), inc="sx", n=16)
        p.op("Y", lambda y, tt=tt: y.dma_start(out=fa[:, f_xr:f_xr + 2048], in_=xres[tt]), inc="sx", n=16)
        lv_ld = p.lvl("sx")
        p.wait("V", "sx", lv_ld)
        if tt >= 1:
            p.wait("V", "sy", lv_x1y[tt - 1])
            p.wait("V", "ss", lv_cast[tt - 1])
        p.op("V", lambda v: v.tensor_tensor(fa[:, f_x1:f_x1 + 2048], fa[:, f_rs:f_rs + 2048],
                                            fa[:, f_xr:f_xr + 2048], op=ALU.add))
        p.op("V", lambda v: v.reduce_sum(fa[:, f_ln:f_ln + 1], fa[:, f_x1:f_x1 + 2048], axis=AX.X),
             inc="sv")
        lv_add = p.lvl("sv")
        p.wait("S", "sv", lv_add)
        p.op("S", lambda s: s.activation(fa[:, f_rs:f_rs + 2048], fa[:, f_x1:f_x1 + 2048], AF.Square),
             inc="ss")
        lv_sq4 = p.lvl("ss")
        p.wait("V", "ss", lv_sq4)
        p.op("V", lambda v: v.reduce_sum(fa[:, f_ln + 1:f_ln + 2], fa[:, f_rs:f_rs + 2048], axis=AX.X),
             inc="gv")
        p.wait("V", "gv")
        p.op("V", lambda v: v.tensor_scalar(fa[:, f_ln + 2:f_ln + 3], fa[:, f_ln:f_ln + 1],
                                            1.0 / 2048, None, op0=ALU.mult))
        p.op("V", lambda v: v.tensor_scalar(fa[:, f_ln + 3:f_ln + 4], fa[:, f_ln + 1:f_ln + 2],
                                            1.0 / 2048, None, op0=ALU.mult), inc="gv")
        p.wait("V", "gv")
        p.op("V", lambda v: v.tensor_tensor(fa[:, f_ln + 4:f_ln + 5], fa[:, f_ln + 2:f_ln + 3],
                                            fa[:, f_ln + 2:f_ln + 3], op=ALU.mult), inc="gv")
        p.wait("V", "gv")
        p.op("V", lambda v: v.tensor_tensor(fa[:, f_ln + 5:f_ln + 6], fa[:, f_ln + 3:f_ln + 4],
                                            fa[:, f_ln + 4:f_ln + 5], op=ALU.subtract), inc="sv")
        p.wait("S", "sv", p.lvl("sv"))
        p.op("S", lambda s: s.activation(fa[:, f_ln + 6:f_ln + 7], fa[:, f_ln + 5:f_ln + 6],
                                         AF.Sqrt, bias=c_eps[:, 0:1]), inc="ss")
        p.wait("V", "ss", p.lvl("ss"))
        p.op("V", lambda v: v.reciprocal(fa[:, f_ln + 7:f_ln + 8], fa[:, f_ln + 6:f_ln + 7]), inc="gv")
        p.wait("V", "gv")
        p.op("V", lambda v: v.tensor_scalar(fa[:, f_x1:f_x1 + 2048], fa[:, f_x1:f_x1 + 2048],
                                            fa[:, f_ln + 2:f_ln + 3], fa[:, f_ln + 7:f_ln + 8],
                                            op0=ALU.subtract, op1=ALU.mult), inc="sv")
        lv_x1 = p.lvl("sv")
        p.wait("Y", "sv", lv_x1)
        p.op("Y", lambda y, tt=tt: y.dma_start(out=x1scr[tt], in_=fa[:, f_x1:f_x1 + 2048]),
             inc="sy", n=16)
        lv_x1y[tt] = p.lvl("sy")
        if DEBUG_TAPS:
            p.op("Y", lambda y, tt=tt: y.dma_start(out=dbg_x1[tt], in_=fa[:, f_x1:f_x1 + 2048]),
                 inc="so", n=16)
        p.wait("S", "sv", lv_x1)
        if tt >= 1:
            p.wait("S", "st", p.lvl("st"))
        p.op("S", lambda s: s.copy(ab[:, o_x1stg:o_x1stg + 2048], fa[:, f_x1:f_x1 + 2048]),
             inc="ss")
        lv_cast[tt] = p.lvl("ss")
        p.wait("T", "ss", lv_cast[tt])
        for kt in range(KT):
            bank = pbf[4] if kt % 2 == 0 else pbf[5]
            if kt >= 2:
                p.wait("T", "ss", p.lvl("ss"))
            p.op("T", lambda t_, kt=kt, bank=bank: t_.transpose(
                bank[:, 0:128], ab[:, o_x1stg + kt * 128:o_x1stg + kt * 128 + 128], c_eye[:]),
                inc="st")
            p.wait("S", "st", p.lvl("st"))
            p.op("S", lambda s, kt=kt, tt=tt, bank=bank: s.copy(
                ab[:, o_x1T + kt * 1024 + tt * 128: o_x1T + kt * 1024 + tt * 128 + 128],
                bank[:, 0:128]), inc="ss")
        lv_x1T_s = p.lvl("ss")
        p.wait("Y", "ss", lv_x1T_s)

    p.wait("T", "sw", lv_p4w)
    p.wait("T", "ss", lv_x1T_s)
    lv_xw_last = 0
    lv_rt_s = {}
    for tt in range(TT):
        if tt >= 1:
            p.wait("T", "ss", lv_rt_s[tt - 1])
        for kt in range(KT):
            p.op("T", lambda t_, tt=tt, kt=kt: t_.matmul(
                pb[6][:, 0:152],
                ab[:, o_x1T + kt * 1024 + tt * 128: o_x1T + kt * 1024 + tt * 128 + 128],
                ab[:, o_wsm + kt * 152: o_wsm + kt * 152 + 152],
                start=(kt == 0), stop=(kt == KT - 1)),
                inc=("sq" if kt == KT - 1 else None))
        lv_rm = p.lvl("sq")
        p.wait("S", "sq", lv_rm)
        if tt >= 1:
            p.wait("S", "sv", lv_xw_last)
        p.op("S", lambda s, tt=tt: s.activation(fa[:, f_rt + tt * 8: f_rt + tt * 8 + 8],
                                                pb[6][:, 0:8], AF.Sigmoid))
        p.op("S", lambda s: s.activation(fa[:, f_xa:f_xa + 72], pb[6][:, 8:80],
                                         AF.Copy, scale=c_8th[:, 0:1]))
        p.op("S", lambda s: s.activation(fa[:, f_xa + 72:f_xa + 144], pb[6][:, 80:152],
                                         AF.Copy, scale=c_8th[:, 0:1]), inc="ss")
        lv_rt_s[tt] = p.lvl("ss")
        p.wait("V", "ss", lv_rt_s[tt])
        rt = f_rt + 64
        sc_t = fa[:, f_rt + tt * 8: f_rt + tt * 8 + 8]
        biased = fa[:, rt:rt + 8]
        m1 = fa[:, rt + 8:rt + 9]
        msk = fa[:, rt + 9:rt + 17]
        m2 = fa[:, rt + 17:rt + 18]
        tw = fa[:, rt + 18:rt + 26]
        s2 = fa[:, rt + 26:rt + 27]
        wr = fa[:, rt + 27:rt + 28]
        wgt = fa[:, f_rt + 104 + tt * 8: f_rt + 104 + tt * 8 + 8]
        p.op("V", lambda v, sc_t=sc_t, biased=biased: v.tensor_tensor(biased, sc_t, c_eb[:], op=ALU.add), inc="gv")
        p.wait("V", "gv")
        p.op("V", lambda v, biased=biased, m1=m1: v.reduce_max(m1, biased, axis=AX.X), inc="gv")
        p.wait("V", "gv")
        p.op("V", lambda v, biased=biased, m1=m1, msk=msk: v.tensor_scalar(
            msk, biased, m1[:, 0:1], None, op0=ALU.is_ge), inc="gv")
        p.wait("V", "gv")
        p.op("V", lambda v, biased=biased, msk=msk: v.scalar_tensor_tensor(
            msk, msk, -1e9, biased, op0=ALU.mult, op1=ALU.add), inc="gv")
        p.wait("V", "gv")
        p.op("V", lambda v, msk=msk, m2=m2: v.reduce_max(m2, msk, axis=AX.X), inc="gv")
        p.wait("V", "gv")
        p.op("V", lambda v, biased=biased, m2=m2, tw=tw: v.tensor_scalar(
            tw, biased, m2[:, 0:1], None, op0=ALU.is_ge), inc="gv")
        p.wait("V", "gv")
        p.op("V", lambda v, tw=tw, sc_t=sc_t: v.tensor_tensor(tw, tw, sc_t, op=ALU.mult), inc="gv")
        p.wait("V", "gv")
        p.op("V", lambda v, tw=tw, s2=s2: v.reduce_sum(s2, tw, axis=AX.X), inc="gv")
        p.wait("V", "gv")
        p.op("V", lambda v, s2=s2: v.tensor_scalar(s2, s2, 1e-8, None, op0=ALU.add), inc="gv")
        p.wait("V", "gv")
        p.op("V", lambda v, s2=s2, wr=wr: v.reciprocal(wr, s2), inc="gv")
        p.wait("V", "gv")
        p.op("V", lambda v, tw=tw, wr=wr, wgt=wgt: v.tensor_scalar(
            wgt, tw, wr[:, 0:1], None, op0=ALU.mult), inc="gv")
        p.wait("V", "gv")
        for which, fsrc, oT in ((0, f_xa, o_xgT), (1, f_xa + 72, o_xuT)):
            stg = ab[:, o_xastg + which * 128: o_xastg + which * 128 + 128]
            p.wait("V", "st", p.lvl("st"))
            p.op("V", lambda v, fsrc=fsrc, stg=stg: v.tensor_copy(stg[:, 0:8], fa[:, fsrc:fsrc + 8]))
            for e in range(8):
                p.op("V", lambda v, fsrc=fsrc, stg=stg, e=e, wgt=wgt: v.tensor_scalar(
                    stg[:, 8 + 8 * e:16 + 8 * e], fa[:, fsrc + 8 + 8 * e:fsrc + 16 + 8 * e],
                    wgt[:, e:e + 1], None, op0=ALU.mult),
                    inc=("sv" if e == 7 else None))
            lv_xw = p.lvl("sv")
            p.wait("T", "sv", lv_xw)
            p.wait("T", "sb2", p.lvl("sb2"))
            p.op("T", lambda t_, stg=stg: t_.transpose(pbf[7][0:72, 0:128], stg[:, 0:72], c_eye[:]),
                 inc="st")
            p.wait("S", "st", p.lvl("st"))
            p.op("S", lambda s, oT=oT, tt=tt: s.copy(
                ab[0:72, oT + tt * 128: oT + tt * 128 + 128], pbf[7][0:72, 0:128]), inc="sb2")
            lv_xw_last = lv_xw
    lv_rtall_v = p.lvl("sv")
    lv_rtall_s = p.lvl("sb2")

    lv_g42 = {}
    it = 0
    for blk in range(2):
        for tau in range(FT):
            slot = it % 2
            if it >= 2:
                p.wait("G", "sq", lv_g42[it - 2][0])
            p.op("G", lambda g, tau=tau, slot=slot: g.dma_start(
                out=ab[:, o_wg + slot * 2048: o_wg + slot * 2048 + 2048], in_=wgt_h[tau]),
                inc="swg", n=16)
            p.op("G", lambda g, tau=tau, slot=slot: g.dma_start(
                out=ab[:, o_wu + slot * 2048: o_wu + slot * 2048 + 2048], in_=wup_h[tau]),
                inc="swg", n=16)
            p.op("G", lambda g, tau=tau, slot=slot: g.dma_start(
                out=ab[0:72, o_bgb + slot * 128: o_bgb + slot * 128 + 128],
                in_=bgd[:, tau * 128:tau * 128 + 128]), inc="swg", n=16)
            p.op("G", lambda g, tau=tau, slot=slot: g.dma_start(
                out=ab[0:72, o_bgb + 256 + slot * 128: o_bgb + 256 + slot * 128 + 128],
                in_=bud[:, tau * 128:tau * 128 + 128]), inc="swg", n=16)
            lv_ld = p.lvl("swg")

            bgt, bup = (pb[0], pb[1]) if it % 2 == 0 else (pb[2], pb[3])
            p.wait("T", "swg", lv_ld)
            if it == 0:
                p.wait("T", "sv", lv_rtall_v)
                p.wait("T", "sb2", lv_rtall_s)
                p.wait("T", "ss", lv_rt_s[TT - 1])
            if it >= 2:
                p.wait("T", "sv", lv_g42[it - 2][2])
                p.wait("T", "ss", lv_g42[it - 2][1])
            for kt in range(KT):
                p.op("T", lambda t_, kt=kt, blk=blk, slot=slot, bgt=bgt: t_.matmul(
                    bgt[:, 0:512],
                    ab[:, o_wg + slot * 2048 + kt * 128: o_wg + slot * 2048 + kt * 128 + 128],
                    ab[:, o_x1T + kt * 1024 + blk * 512: o_x1T + kt * 1024 + blk * 512 + 512],
                    start=(kt == 0), stop=False, skip_group_check=True))
            p.op("T", lambda t_, blk=blk, slot=slot, bgt=bgt: t_.matmul(
                bgt[:, 0:512],
                ab[0:72, o_bgb + slot * 128: o_bgb + slot * 128 + 128],
                ab[0:72, o_xgT + blk * 512: o_xgT + blk * 512 + 512],
                start=False, stop=True, skip_group_check=True))
            for kt in range(KT):
                p.op("T", lambda t_, kt=kt, blk=blk, slot=slot, bup=bup: t_.matmul(
                    bup[:, 0:512],
                    ab[:, o_wu + slot * 2048 + kt * 128: o_wu + slot * 2048 + kt * 128 + 128],
                    ab[:, o_x1T + kt * 1024 + blk * 512: o_x1T + kt * 1024 + blk * 512 + 512],
                    start=(kt == 0), stop=False, skip_group_check=True))
            p.op("T", lambda t_, blk=blk, slot=slot, bup=bup: t_.matmul(
                bup[:, 0:512],
                ab[0:72, o_bgb + 256 + slot * 128: o_bgb + 256 + slot * 128 + 128],
                ab[0:72, o_xuT + blk * 512: o_xuT + blk * 512 + 512],
                start=False, stop=True, skip_group_check=True), inc="sq")
            lv_gu = p.lvl("sq")
            p.wait("S", "sq", lv_gu)
            if it >= 2:
                p.wait("S", "sv", lv_g42[it - 2][2])
            p.op("S", lambda s, slot=slot, bgt=bgt: s.activation(
                fa[:, f_tsq + slot * 512: f_tsq + slot * 512 + 512], bgt[:, 0:512], AF.Silu),
                inc="ss")
            lv_sil = p.lvl("ss")
            p.wait("V", "ss", lv_sil)
            p.op("V", lambda v, tau=tau, blk=blk, slot=slot, bup=bup: v.tensor_tensor(
                ab[:, o_hid + tau * 1024 + blk * 512: o_hid + tau * 1024 + blk * 512 + 512],
                fa[:, f_tsq + slot * 512: f_tsq + slot * 512 + 512],
                bup[:, 0:512], op=ALU.mult), inc="sv")
            lv_g42[it] = (lv_gu, lv_sil, p.lvl("sv"))
            it += 1
    lv_hid_all = p.lvl("sv")
    lv_gu_all = p.lvl("sq")
    if DEBUG_TAPS:
        p.wait("Y", "sv", lv_hid_all)
        p.op("Y", lambda y: y.dma_start(out=dbg_hid, in_=ab[:, o_hid:o_hid + FT * 1024]),
             inc="so", n=16)
        p.op("Y", lambda y: y.dma_start(out=dbg_wgt, in_=fa[:, f_rt + 104:f_rt + 104 + 64]),
             inc="so", n=16)

    p.wait("T", "sv", lv_hid_all)
    lv_xd_s = {}
    for tt in range(TT):
        if tt >= 1:
            p.wait("T", "ss", lv_xd_s[tt - 1])
        for tau in range(FT):
            p.op("T", lambda t_, tt=tt, tau=tau: t_.matmul(
                pb[6][:, 0:72],
                ab[:, o_hid + tau * 1024 + tt * 128: o_hid + tau * 1024 + tt * 128 + 128],
                ab[:, o_adn + tau * 72: o_adn + tau * 72 + 72],
                start=(tau == 0), stop=(tau == FT - 1)),
                inc=("sq" if tau == FT - 1 else None))
        lv_xd = p.lvl("sq")
        p.wait("S", "sq", lv_xd)
        if tt >= 1:
            p.wait("S", "sv", p.lvl("sv"))
        p.op("S", lambda s: s.activation(fa[:, f_xa + 144:f_xa + 216], pb[6][:, 0:72],
                                         AF.Copy, scale=c_8th[:, 0:1]), inc="ss")
        lv_xd_s[tt] = p.lvl("ss")
        p.wait("V", "ss", lv_xd_s[tt])
        p.wait("V", "st", p.lvl("st"))
        stg = ab[:, o_xastg:o_xastg + 128]
        p.op("V", lambda v, stg=stg: v.tensor_copy(stg[:, 0:8], fa[:, f_xa + 144:f_xa + 152]))
        for e in range(8):
            p.op("V", lambda v, stg=stg, e=e, tt=tt: v.tensor_scalar(
                stg[:, 8 + 8 * e:16 + 8 * e], fa[:, f_xa + 152 + 8 * e:f_xa + 160 + 8 * e],
                fa[:, f_rt + 104 + tt * 8 + e: f_rt + 104 + tt * 8 + e + 1], None,
                op0=ALU.mult),
                inc=("sv" if e == 7 else None))
        lv_xw = p.lvl("sv")
        p.wait("T", "sv", lv_xw)
        p.wait("T", "sb2", p.lvl("sb2"))
        p.op("T", lambda t_, stg=stg: t_.transpose(pbf[7][0:72, 0:128], stg[:, 0:72], c_eye[:]),
             inc="st")
        p.wait("S", "st", p.lvl("st"))
        p.op("S", lambda s, tt=tt: s.copy(ab[0:72, o_xdT + tt * 128: o_xdT + tt * 128 + 128],
                                          pbf[7][0:72, 0:128]), inc="sb2")
    lv_xdT = p.lvl("sb2")

    p.wait("T", "sb2", lv_xdT)
    p.wait("Y", "sy", lv_x1y[TT - 1])
    p.wait("Y", "sv", p.lvl("sv"))
    lv_d_ld = {}
    lv_d_ev = {}
    lv_d_vv = {}
    lv_d_y = {}
    for db in range(4):
        for tau in range(FT):
            gk = db * FT + tau
            slot = gk % 2
            if gk >= 2:
                p.wait("G", "sq", lv_d_ld[gk - 2][1])
            elif db == 0 and tau == 0:
                p.wait("G", "sq", lv_gu_all)
            p.op("G", lambda g, tau=tau, slot=slot: g.dma_start(
                out=ab[:, o_wd + slot * 2048: o_wd + slot * 2048 + 2048], in_=wdn_h[tau]),
                inc="swd", n=16)
            lv_ld = p.lvl("swd")
            p.wait("T", "swd", lv_ld)
            if tau == 0 and db > 0:
                p.wait("T", "sv", lv_d_ev[db - 1])
            for tt in range(TT):
                p.op("T", lambda t_, tt=tt, tau=tau, db=db, slot=slot: t_.matmul(
                    pb[tt][:, 0:512],
                    ab[:, o_hid + tau * 1024 + tt * 128: o_hid + tau * 1024 + tt * 128 + 128],
                    ab[:, o_wd + slot * 2048 + db * 512: o_wd + slot * 2048 + db * 512 + 512],
                    start=(tau == 0), stop=False, skip_group_check=True),
                    inc=("sq" if tt == TT - 1 else None))
            lv_d_ld[gk] = (lv_ld, p.lvl("sq"))
        for tt in range(TT):
            kk = db * TT + tt
            p.op("T", lambda t_, tt=tt, db=db: t_.matmul(
                pb[tt][:, 0:512],
                ab[0:72, o_xdT + tt * 128: o_xdT + tt * 128 + 128],
                ab[0:72, o_bd + db * 512: o_bd + db * 512 + 512],
                start=False, stop=True, skip_group_check=True), inc="sq")
            lv_cl = p.lvl("sq")
            slot2 = tt % 2
            if kk >= 2:
                p.wait("Y", "sv", lv_d_vv[kk - 2])
            p.op("Y", lambda y, tt=tt, db=db, slot2=slot2: y.dma_start(
                out=fa[:, f_x1d + slot2 * 512: f_x1d + slot2 * 512 + 512],
                in_=x1scr[tt][:, db * 512:db * 512 + 512]), inc="sx", n=16)
            lv_x1l = p.lvl("sx")
            p.wait("V", "sq", lv_cl)
            p.wait("V", "sx", lv_x1l)
            if kk >= 2:
                p.wait("V", "sy", lv_d_y[kk - 2])
            p.op("V", lambda v, tt=tt, slot2=slot2: v.tensor_tensor(
                fa[:, f_stg + slot2 * 512: f_stg + slot2 * 512 + 512],
                fa[:, f_x1d + slot2 * 512: f_x1d + slot2 * 512 + 512],
                pb[tt][:, 0:512], op=ALU.add), inc="sv")
            lv_d_vv[kk] = p.lvl("sv")
            p.wait("Y", "sv", lv_d_vv[kk])
            p.op("Y", lambda y, tt=tt, db=db, slot2=slot2: y.dma_start(
                out=dscr[tt][:, db * 512:db * 512 + 512],
                in_=fa[:, f_stg + slot2 * 512: f_stg + slot2 * 512 + 512]), inc="sy", n=16)
            lv_d_y[kk] = p.lvl("sy")
        lv_d_ev[db] = p.lvl("sv")
    lv_dn_all = p.lvl("sy")

    p.wait("Y", "sy", lv_dn_all)
    if DEBUG_TAPS:
        p.op("Y", lambda y: y.dma_start(out=dbg_ds, in_=dscr), inc="so", n=16)
    lv_out_y = {}
    for tt in range(TT):
        p.op("Y", lambda y, tt=tt: y.dma_start(out=fa[:, f_df:f_df + 2048], in_=dscr[tt]),
             inc="sx", n=16)
        lv_ld = p.lvl("sx")
        p.wait("V", "sx", lv_ld)
        p.op("V", lambda v: v.reduce_sum(fa[:, f_ln:f_ln + 1], fa[:, f_df:f_df + 2048], axis=AX.X),
             inc="sv")
        p.wait("S", "sv", p.lvl("sv"))
        p.op("S", lambda s: s.activation(fa[:, f_x1d:f_x1d + 2048], fa[:, f_df:f_df + 2048],
                                         AF.Square), inc="ss")
        p.wait("V", "ss", p.lvl("ss"))
        p.op("V", lambda v: v.reduce_sum(fa[:, f_ln + 1:f_ln + 2], fa[:, f_x1d:f_x1d + 2048],
                                         axis=AX.X), inc="gv")
        p.wait("V", "gv")
        p.op("V", lambda v: v.tensor_scalar(fa[:, f_ln + 2:f_ln + 3], fa[:, f_ln:f_ln + 1],
                                            1.0 / 2048, None, op0=ALU.mult))
        p.op("V", lambda v: v.tensor_scalar(fa[:, f_ln + 3:f_ln + 4], fa[:, f_ln + 1:f_ln + 2],
                                            1.0 / 2048, None, op0=ALU.mult), inc="gv")
        p.wait("V", "gv")
        p.op("V", lambda v: v.tensor_tensor(fa[:, f_ln + 4:f_ln + 5], fa[:, f_ln + 2:f_ln + 3],
                                            fa[:, f_ln + 2:f_ln + 3], op=ALU.mult), inc="gv")
        p.wait("V", "gv")
        p.op("V", lambda v: v.tensor_tensor(fa[:, f_ln + 5:f_ln + 6], fa[:, f_ln + 3:f_ln + 4],
                                            fa[:, f_ln + 4:f_ln + 5], op=ALU.subtract), inc="sv")
        p.wait("S", "sv", p.lvl("sv"))
        p.op("S", lambda s: s.activation(fa[:, f_ln + 6:f_ln + 7], fa[:, f_ln + 5:f_ln + 6],
                                         AF.Sqrt, bias=c_eps[:, 0:1]), inc="ss")
        p.wait("V", "ss", p.lvl("ss"))
        p.op("V", lambda v: v.reciprocal(fa[:, f_ln + 7:f_ln + 8], fa[:, f_ln + 6:f_ln + 7]),
             inc="gv")
        p.wait("V", "gv")
        if tt >= 1:
            p.wait("V", "so", lv_out_y[tt - 1])
        p.op("V", lambda v: v.tensor_scalar(fa[:, f_os:f_os + 2048], fa[:, f_df:f_df + 2048],
                                            fa[:, f_ln + 2:f_ln + 3], fa[:, f_ln + 7:f_ln + 8],
                                            op0=ALU.subtract, op1=ALU.mult), inc="sv")
        lv_o = p.lvl("sv")
        p.wait("Y", "sv", lv_o)
        p.op("Y", lambda y, tt=tt: y.dma_start(out=out[tt], in_=fa[:, f_os:f_os + 2048]),
             inc="so", n=16)
        lv_out_y[tt] = p.lvl("so")

    p.wait("Y", "so", p.lvl("so"))

    with nc.Block() as block:
        @block.tensor
        def _(e):
            for fn in p.q["T"]:
                fn(e)

        @block.vector
        def _(e):
            for fn in p.q["V"]:
                fn(e)

        @block.scalar
        def _(e):
            for fn in p.q["S"]:
                fn(e)

        @block.gpsimd
        def _(e):
            for fn in p.q["G"]:
                fn(e)

        @block.sync
        def _(e):
            for fn in p.q["Y"]:
                fn(e)

    ctx.close()
    return nc


def _c(a):
    return np.ascontiguousarray(a, dtype=np.float32)


def prep_inputs(inp):
    x = np.asarray(inp["L0_attn_x"], np.float32)
    wqkv = np.asarray(inp["L0_attn_w_qkv"], np.float32)
    wout = np.asarray(inp["L0_attn_w_out"], np.float32)
    lnsc = np.asarray(inp["L0_attn_ln_scale"], np.float32)
    lnbi = np.asarray(inp["L0_attn_ln_bias"], np.float32)
    Wr = np.asarray(inp["L0_ffn_W_router"], np.float32)
    Wg = np.asarray(inp["L0_ffn_W_gate"], np.float32)
    Wu = np.asarray(inp["L0_ffn_W_up"], np.float32)
    Wd = np.asarray(inp["L0_ffn_W_down"], np.float32)
    sAg = np.asarray(inp["L0_ffn_sA_gate"], np.float32)
    sBg = np.asarray(inp["L0_ffn_sB_gate"], np.float32)
    sAu = np.asarray(inp["L0_ffn_sA_up"], np.float32)
    sBu = np.asarray(inp["L0_ffn_sB_up"], np.float32)
    sAd = np.asarray(inp["L0_ffn_sA_down"], np.float32)
    sBd = np.asarray(inp["L0_ffn_sB_down"], np.float32)
    Ag = np.asarray(inp["L0_ffn_A_gate"], np.float32)
    Bg = np.asarray(inp["L0_ffn_B_gate"], np.float32)
    Au = np.asarray(inp["L0_ffn_A_up"], np.float32)
    Bu = np.asarray(inp["L0_ffn_B_up"], np.float32)
    Ad = np.asarray(inp["L0_ffn_A_down"], np.float32)
    Bd = np.asarray(inp["L0_ffn_B_down"], np.float32)
    eb = np.asarray(inp["L0_ffn_expert_bias"], np.float32)

    pos = np.arange(4096, dtype=np.float64)
    invf = 1.0 / (10000.0 ** (np.arange(0, 128, 2, dtype=np.float64) / 128.0))
    ang = pos[:, None] * invf[None, :]
    cos = np.cos(ang).astype(np.float32)
    sin = np.sin(ang).astype(np.float32)
    ropeC = _c(cos.reshape(32, 128, 64).transpose(1, 0, 2).reshape(128, 2048))
    ropeS = _c(sin.reshape(32, 128, 64).transpose(1, 0, 2).reshape(128, 2048))
    lncol = _c(np.stack([lnsc, lnbi], axis=1))
    ebrow = _c(eb.reshape(1, 8))
    maskst = _c((np.arange(128)[:, None] <= np.arange(896)[None, :] - 384).astype(np.float32))
    eyebf = np.eye(128, dtype=np.float32).astype(BF)

    Agc = np.concatenate([sAg] + [Ag[e] for e in range(8)], axis=1)
    Auc = np.concatenate([sAu] + [Au[e] for e in range(8)], axis=1)
    Adc = np.concatenate([sAd] + [Ad[e] for e in range(8)], axis=1)
    Bgc = np.concatenate([sBg, Bg.reshape(64, DFF)], axis=0)
    Buc = np.concatenate([sBu, Bu.reshape(64, DFF)], axis=0)
    Bdc = np.concatenate([sBd, Bd.reshape(64, 2048)], axis=0)
    wsm_full = np.concatenate([Wr, Agc, Auc], axis=1)
    wsm = _c(wsm_full.reshape(16, 128, 152).transpose(1, 0, 2).reshape(128, 16 * 152))
    wgt_h = _c(Wg.reshape(16, 128, 43, 128).transpose(2, 1, 0, 3).reshape(43, 128, 2048))
    wup_h = _c(Wu.reshape(16, 128, 43, 128).transpose(2, 1, 0, 3).reshape(43, 128, 2048))
    wdn_h = _c(Wd.reshape(43, 128, 2048))
    adn = _c(Adc.reshape(43, 128, 72).transpose(1, 0, 2).reshape(128, 43 * 72))
    bg_ = _c(Bgc)
    bu_ = _c(Buc)
    bd_ = _c(Bdc)

    maps = []
    for c in range(8):
        g, loc = c // 4, c % 4
        xg = x[g]
        xTh = _c(xg.reshape(32, 128, 16, 128).transpose(0, 3, 2, 1).reshape(32, 128, 2048))
        wq = wqkv[:, 512 * loc:512 * loc + 512]
        wk = wqkv[:, 2048 + 512 * loc:2048 + 512 * loc + 512]
        wv_ = wqkv[:, 4096 + 512 * loc:4096 + 512 * loc + 512]
        wqk_i = np.empty((2048, 1024), np.float32)
        wqk_i[:, 0:512] = wq
        wqk_i[:, 512:1024] = wk
        wqk_h = _c(wqk_i.reshape(16, 128, 1024).transpose(1, 0, 2).reshape(128, 16384))
        wv_h = _c(wv_.reshape(16, 128, 512).transpose(1, 0, 2).reshape(128, 8192))
        xres_ = _c(xg[1024 * loc:1024 * loc + 1024].reshape(8, 128, 2048))
        woth_ = _c(wout[512 * loc:512 * loc + 512, :].reshape(4, 128, 2048)
                   .transpose(1, 0, 2).reshape(128, 8192))
        maps.append(dict(
            xTh=xTh, wqk=wqk_h, wv=wv_h, ropeC=ropeC, ropeS=ropeS, lncol=lncol,
            ebrow=ebrow, maskst=maskst, eyebf=eyebf, xres=xres_, woth=woth_,
            wsm=wsm, wgt_h=wgt_h, wup_h=wup_h, wdn_h=wdn_h, bgd=bg_, bud=bu_,
            bdd=bd_, adn=adn))
    return maps


def kernel(**inputs):
    maps = prep_inputs(inputs)
    nc = build_program()
    res = run_bass_kernel_spmd(nc, maps, core_ids=list(range(8)))
    out = np.empty((2, 4096, 2048), np.float32)
    for c in range(8):
        g, loc = c // 4, c % 4
        o = np.asarray(res.results[c]["out"], np.float32).reshape(1024, 2048)
        out[g, 1024 * loc:1024 * loc + 1024, :] = o
    return out


# version 09 (final)
# speedup vs baseline: 1.5206x; 1.5206x over previous
import numpy as np
import ml_dtypes
import concourse.bass as bass
import concourse.mybir as mybir
from concourse.bass_utils import run_bass_kernel_spmd

F32 = mybir.dt.float32
BF16 = mybir.dt.bfloat16
AF = mybir.ActivationFunctionType
ALU = mybir.AluOpType
AX = mybir.AxisListType
BF = ml_dtypes.bfloat16

EPS = 1e-5
DFF = 5504
FT = 43
KT = 16
NT = 32
NBLK = 8
TT = 8

DEBUG_TAPS = False
DEBUG_TAPS2 = False


class Prog:

    def __init__(self, nc, sem_names):
        self.nc = nc
        self.q = {k: [] for k in "TVSGY"}
        self.c = {n: 0 for n in sem_names}
        self.sems = {}
        self._cms = []
        for n in sem_names:
            cm = nc.semaphore(n)
            self._cms.append(cm)
            self.sems[n] = cm.__enter__()

    def op(self, e, fn, inc=None, n=1):
        if inc is None:
            self.q[e].append(fn)
        else:
            self.c[inc] += n
            s = self.sems[inc]
            self.q[e].append(lambda eng, fn=fn, s=s, n=n: fn(eng).then_inc(s, n))

    def wait(self, e, sem, val=None):
        v = self.c[sem] if val is None else val
        assert 0 <= v <= self.c[sem], f"wait on future value of {sem}: {v} > {self.c[sem]}"
        if v == 0:
            return
        s = self.sems[sem]
        self.q[e].append(lambda eng, s=s, v=v: eng.wait_ge(s, v))

    def lvl(self, sem):
        return self.c[sem]


def build_program():
    nc = bass.Bass("TRN2", target_bir_lowering=False)
    dram = lambda name, shape, dt, kind: nc.dram_tensor(name, shape, dt, kind=kind).ap()
    EI, EO, IN = "ExternalInput", "ExternalOutput", "Internal"

    xTh = dram("xTh", [NT, 128, 2048], F32, EI)
    wqk = dram("wqk", [128, KT * 1024], F32, EI)
    wv = dram("wv", [128, KT * 512], F32, EI)
    ropeC = dram("ropeC", [128, NT * 64], F32, EI)
    ropeS = dram("ropeS", [128, NT * 64], F32, EI)
    lncol = dram("lncol", [128, 2], F32, EI)
    ebrow = dram("ebrow", [1, 8], F32, EI)
    maskst = dram("maskst", [128, 896], F32, EI)
    eyebf = dram("eyebf", [128, 128], BF16, EI)
    xres = dram("xres", [TT, 128, 2048], F32, EI)
    woth = dram("woth", [128, 4 * 2048], F32, EI)
    wsm = dram("wsm", [128, KT * 152], F32, EI)
    wgt_h = dram("wgt_h", [FT, 128, 2048], F32, EI)
    wup_h = dram("wup_h", [FT, 128, 2048], F32, EI)
    wdn_h = dram("wdn_h", [4, FT, 128, 512], F32, EI)
    bgd = dram("bgd", [FT, 72, 128], F32, EI)
    bud = dram("bud", [FT, 72, 128], F32, EI)
    bdd = dram("bdd", [72, 2048], F32, EI)
    adn = dram("adn", [128, FT * 72], F32, EI)
    avdr = dram("avdr", [4, 128, 4096], BF16, IN)
    bar_i = dram("bar_i", [128], F32, IN)
    bar_o = dram("bar_o", [4 * 128], F32, IN)
    partial = dram("partial", [NT, 128, 2048], F32, IN)
    rsout = dram("rsout", [TT * 128 * 2048], F32, IN)
    x1scr = dram("x1scr", [TT, 4, 128, 512], F32, IN)
    dscr = dram("dscr", [TT, 128, 2048], F32, IN)
    out = dram("out", [TT, 128, 2048], F32, EO)
    if DEBUG_TAPS2:
        dbg_rs2 = dram("dbg_rs2", [TT * 128 * 2048], F32, EO)
        dbg_x12 = dram("dbg_x12", [TT, 4, 128, 512], F32, EO)
        dbg_pa2 = dram("dbg_pa2", [NT, 128, 2048], F32, EO)
        dbg_ds2 = dram("dbg_ds2", [TT, 128, 2048], F32, EO)
        dbg_wg2 = dram("dbg_wg2", [128, 64], F32, EO)
        dbg_h2 = dram("dbg_h2", [128, FT * 1024], BF16, EO)
        dbg_xt2 = dram("dbg_xt2", [128, KT * 1024], BF16, EO)
        dbg_xa2 = dram("dbg_xa2", [128, 3072], BF16, EO)
    if DEBUG_TAPS:
        dbg_qk = dram("dbg_qk", [128, 8 * 4096], BF16, EO)
        dbg_v = dram("dbg_v", [128, 4 * 4096], BF16, EO)
        dbg_av = dram("dbg_av", [4, 128, 4096], BF16, EO)
        dbg_x1 = dram("dbg_x1", [TT, 128, 2048], F32, EO)
        dbg_wgt = dram("dbg_wgt", [128, 64], F32, EO)
        dbg_hid = dram("dbg_hid", [128, FT * 1024], BF16, EO)
        dbg_ds = dram("dbg_ds", [TT, 128, 2048], F32, EO)

    from contextlib import ExitStack
    ctx = ExitStack()
    sbt = lambda name, shape, dt: ctx.enter_context(nc.sbuf_tensor(name, shape, dt)).ap()
    psn = lambda name: ctx.enter_context(nc.psum_tensor(name, [128, 512], F32)).ap()

    o_wqk = 0
    o_wv = 0
    o_xb = 16384
    o_qkT = 20480
    o_v = 53248
    o_p = 69632
    o_avs = 71168
    o_qtm = 72192
    o_wo = 0
    o_ava = 8192
    o_wg = 0
    o_wu = 4096
    o_wd = 0
    o_wsm = 8192
    o_xgT = 10624
    o_xuT = 11648
    o_xdT = 12672
    o_x1stg = 13696
    o_xastg = 15744
    o_bgb = 16000
    o_bd = 16512
    o_adn = 18560
    o_x1T = 21760
    o_hid = 38144
    AB = 82176
    ab = sbt("ab", [128, AB], BF16)
    fa = sbt("fa", [128, 6944], F32)

    c_lncol = sbt("c_lncol", [128, 2], F32)
    c_eb = sbt("c_eb", [128, 8], F32)
    c_eye = sbt("c_eye", [128, 128], BF16)
    c_1bf = sbt("c_1bf", [128, 1], BF16)
    c_1row = sbt("c_1row", [1, 128], F32)
    c_eps = sbt("c_eps", [128, 1], F32)
    c_8th = sbt("c_8th", [128, 1], F32)
    eb_st = sbt("eb_st", [1, 8], F32)

    f_rq = 0
    f_t2 = 1024
    f_bst = 1536
    f_mv = 1600
    f_sd = 1632
    f_rr = 1648
    f_cos = 1664
    f_sin = 3712
    f_tsq = 0
    f_den = 1536
    f_avf = 2560
    f_msk = 3072
    f_prt = 0
    f_rs = 0
    f_xr = 2048
    f_x1 = 4096
    f_x1d = 0
    f_stg = 1024
    f_df = 2048
    f_os = 4096
    f_ln = 6144
    f_rt = 6272
    f_xa = 6720

    pb = [psn(f"pb{i}") for i in range(8)]
    pbf = [b.bitcast(BF16) for b in pb]

    sems = ["sx", "sw", "sq", "st", "sv", "ss", "gv", "sy", "scc", "so",
            "swg", "swd", "sb2"]
    p = Prog(nc, sems)

    p.op("G", lambda g: g.dma_start(out=fa[:, f_cos:f_cos + 2048], in_=ropeC), inc="sw", n=16)
    p.op("G", lambda g: g.dma_start(out=fa[:, f_sin:f_sin + 2048], in_=ropeS), inc="sw", n=16)
    p.op("G", lambda g: g.dma_start(out=c_eye[:], in_=eyebf), inc="sw", n=16)
    p.op("G", lambda g: g.dma_start(out=c_lncol[:], in_=lncol), inc="sw", n=16)
    p.op("G", lambda g: g.dma_start(out=eb_st[:], in_=ebrow), inc="sw", n=16)
    p.op("G", lambda g: g.dma_start(out=ab[:, o_wqk:o_wqk + 16384], in_=wqk), inc="sw", n=16)
    lv_const_w = p.lvl("sw")
    p.op("V", lambda v: v.memset(c_1bf[:], 1.0))
    p.op("V", lambda v: v.memset(c_1row[:], 1.0))
    p.op("V", lambda v: v.memset(c_eps[:], EPS))
    p.op("V", lambda v: v.memset(c_8th[:], 0.125), inc="sv")
    lv_const_v = p.lvl("sv")
    p.wait("T", "sw", lv_const_w)
    p.wait("T", "sv", lv_const_v)
    p.op("T", lambda t: t.matmul(pb[6][:, 0:8], c_1row[:], eb_st[0:1, :], start=True, stop=True),
         inc="sq")
    lv_bc = p.lvl("sq")
    p.wait("S", "sq", lv_bc)
    p.op("S", lambda s: s.copy(c_eb[:], pb[6][:, 0:8]), inc="ss")
    lv_const_s = p.lvl("ss")

    lv_sq1, lv_svr, lv_st1, lv_ss1 = {}, {}, {}, {}

    def xslot(i):
        o = o_xb + (i % 2) * 2048
        return ab[:, o:o + 2048]

    for t in range(NT):
        if t >= 2:
            p.wait("G", "sq", lv_sq1[t - 2])
        p.op("G", lambda y, t=t: y.dma_start(out=xslot(t), in_=xTh[t]), inc="sx", n=16)
        lv_x = p.lvl("sx")

        p.wait("T", "sx", lv_x)
        if t >= 2:
            p.wait("T", "sv", lv_svr[t - 2])
        bq, bk = (pb[0], pb[1]) if t % 2 == 0 else (pb[2], pb[3])
        for cb, bank in ((0, bq), (1, bk)):
            for kt in range(KT):
                p.op("T", lambda t_, t=t, kt=kt, cb=cb, bank=bank: t_.matmul(
                    bank[:, 0:512],
                    xslot(t)[:, kt * 128:kt * 128 + 128],
                    ab[:, o_wqk + kt * 1024 + cb * 512: o_wqk + kt * 1024 + cb * 512 + 512],
                    start=(kt == 0), stop=(kt == KT - 1)),
                    inc=("sq" if kt == KT - 1 and cb == 1 else None))
        lv_sq1[t] = p.lvl("sq")

        p.wait("V", "sq", lv_sq1[t])
        if t == 0:
            p.wait("V", "sw", lv_const_w)
        cosT = fa[:, f_cos + t * 64:f_cos + t * 64 + 64]
        sinT = fa[:, f_sin + t * 64:f_sin + t * 64 + 64]

        def _ups(u, bq=bq, bk=bk):
            bank = bq if u % 2 == 0 else bk
            h = u // 2
            return bank[:, h * 128:h * 128 + 128]

        psl = [_ups(u) for u in range(8)]
        rq = lambda u, a, b: fa[:, f_rq + u * 128 + a: f_rq + u * 128 + b]
        t2 = lambda u: fa[:, f_t2 + u * 64: f_t2 + u * 64 + 64]
        if t >= 2:
            p.wait("V", "st", lv_st1[t - 2])
        for u in range(8):
            p.op("V", lambda v, u=u, ps_=psl[u], cosT=cosT: v.tensor_tensor(rq(u, 0, 64), ps_[:, 0:64], cosT, op=ALU.mult))
        for u in range(8):
            p.op("V", lambda v, u=u, ps_=psl[u], sinT=sinT: v.tensor_tensor(t2(u), ps_[:, 64:128], sinT, op=ALU.mult))
        for u in range(8):
            p.op("V", lambda v, u=u: v.tensor_tensor(rq(u, 0, 64), rq(u, 0, 64), t2(u), op=ALU.subtract))
        for u in range(8):
            p.op("V", lambda v, u=u, ps_=psl[u], sinT=sinT: v.tensor_tensor(t2(u), ps_[:, 0:64], sinT, op=ALU.mult))
        for u in range(8):
            p.op("V", lambda v, u=u, ps_=psl[u], cosT=cosT: v.tensor_tensor(rq(u, 64, 128), ps_[:, 64:128], cosT, op=ALU.mult))
        for u in range(8):
            p.op("V", lambda v, u=u: v.tensor_tensor(rq(u, 64, 128), t2(u), rq(u, 64, 128), op=ALU.add))
        for u in range(8):
            p.op("V", lambda v, u=u: v.bn_stats(fa[:, f_bst + u * 6:f_bst + u * 6 + 6], rq(u, 0, 128)))
        for u in range(8):
            p.op("V", lambda v, u=u: v.bn_aggr(fa[:, f_mv + u * 2:f_mv + u * 2 + 2],
                                               fa[:, f_bst + u * 6:f_bst + u * 6 + 6]),
                 inc=("sv" if u == 7 else None))
        lv_stats = p.lvl("sv")
        p.wait("S", "sv", lv_stats)
        p.op("S", lambda s: s.activation(fa[:, f_sd:f_sd + 8], fa[:, f_mv + 1:f_mv + 16:2],
                                         AF.Sqrt, bias=c_eps[:, 0:1]), inc="ss")
        lv_sd = p.lvl("ss")
        p.wait("V", "ss", lv_sd)
        p.op("V", lambda v: v.reciprocal(fa[:, f_rr:f_rr + 8], fa[:, f_sd:f_sd + 8]), inc="gv")
        p.wait("V", "gv")
        qsl = o_qtm + (t % 2) * 1024
        for u in range(8):
            p.op("V", lambda v, u=u, qsl=qsl: v.tensor_scalar(
                ab[:, qsl + u * 128:qsl + u * 128 + 128],
                rq(u, 0, 128),
                fa[:, f_mv + u * 2:f_mv + u * 2 + 1], fa[:, f_rr + u:f_rr + u + 1],
                op0=ALU.subtract, op1=ALU.mult),
                inc=("sv" if u == 7 else None))
        lv_svr[t] = p.lvl("sv")

        def emit_tp1a(tp):
            p.wait("T", "sv", lv_svr[tp])
            if tp >= 1:
                p.wait("T", "ss", lv_ss1[tp - 1])
            qs2 = o_qtm + (tp % 2) * 1024
            for u in range(8):
                bank = pbf[4] if u < 4 else pbf[5]
                p.op("T", lambda t_, u=u, bank=bank, qs2=qs2: t_.transpose(
                    bank[:, (u % 4) * 128:(u % 4) * 128 + 128],
                    ab[:, qs2 + u * 128:qs2 + u * 128 + 128], c_eye[:]),
                    inc=("st" if u == 7 else None))
            lv_st1[tp] = p.lvl("st")
            p.wait("S", "st", lv_st1[tp])
            if tp == 0:
                p.wait("S", "sw", lv_const_w)
            for u in range(8):
                bank = pbf[4] if u < 4 else pbf[5]
                p.op("S", lambda s, u=u, tp=tp, bank=bank: s.activation(
                    ab[:, o_qkT + u * 4096 + tp * 128: o_qkT + u * 4096 + tp * 128 + 128],
                    bank[:, (u % 4) * 128:(u % 4) * 128 + 128],
                    AF.Identity, bias=c_lncol[:, 1:2], scale=c_lncol[:, 0:1]),
                    inc=("ss" if u == 7 else None))
            lv_ss1[tp] = p.lvl("ss")

        if t >= 1:
            emit_tp1a(t - 1)
        if t == NT - 1:
            emit_tp1a(t)

    p.wait("G", "sq", lv_sq1[NT - 1])
    p.op("G", lambda g: g.dma_start(out=ab[:, o_wv:o_wv + 8192], in_=wv), inc="sw", n=16)
    lv_wv = p.lvl("sw")
    p.wait("G", "sv", lv_svr[NT - 1])
    p.op("G", lambda g: g.dma_start(out=fa[:, f_msk:f_msk + 896], in_=maskst), inc="sw", n=16)
    lv_msk = p.lvl("sw")
    lv_sqv, lv_ssv = {}, {}
    for t in range(NT):
        p.wait("G", "sq", lv_sqv[t - 2] if t >= 2 else lv_sq1[NT - 1])
        p.op("G", lambda y, t=t: y.dma_start(out=xslot(t), in_=xTh[t]), inc="sx", n=16)
        lv_x = p.lvl("sx")
        p.wait("T", "sx", lv_x)
        if t == 0:
            p.wait("T", "sw", lv_wv)
            p.wait("T", "ss", lv_const_s)
        bankv = pb[6] if t % 2 == 0 else pb[7]
        if t >= 2:
            p.wait("T", "ss", lv_ssv[t - 2])
        for kt in range(KT):
            p.op("T", lambda t_, t=t, kt=kt, bankv=bankv: t_.matmul(
                bankv[:, 0:512],
                xslot(t)[:, kt * 128:kt * 128 + 128],
                ab[:, o_wv + kt * 512:o_wv + kt * 512 + 512],
                start=(kt == 0), stop=(kt == KT - 1)),
                inc=("sq" if kt == KT - 1 else None))
        lv_sqv[t] = p.lvl("sq")
        p.wait("S", "sq", lv_sqv[t])
        for h in range(4):
            p.op("S", lambda s, t=t, h=h, bankv=bankv: s.copy(
                ab[:, o_v + h * 4096 + t * 128: o_v + h * 4096 + t * 128 + 128],
                bankv[:, h * 128:h * 128 + 128]),
                inc=("ss" if h == 3 else None))
        lv_ssv[t] = p.lvl("ss")

    if DEBUG_TAPS:
        p.wait("Y", "ss", lv_ss1[NT - 1])
        p.wait("Y", "ss", lv_ssv[NT - 1])
        p.op("Y", lambda y: y.dma_start(out=dbg_qk, in_=ab[:, o_qkT:o_qkT + 8 * 4096]), inc="so", n=16)
        p.op("Y", lambda y: y.dma_start(out=dbg_v, in_=ab[:, o_v:o_v + 4 * 4096]), inc="so", n=16)

    p.wait("T", "ss", lv_ss1[NT - 1])
    p.wait("T", "ss", lv_ssv[NT - 1])
    p.wait("V", "sw", lv_msk)
    qof = lambda h: o_qkT + (2 * h) * 4096
    kof = lambda h: o_qkT + (2 * h + 1) * 4096

    g_ssq = {}
    g_p = {}
    g_av = {}
    g_close = {}
    lv_ep_s = {}
    lv_ep_rec = {}
    lv_ep_bc = {}
    lv_ep_v = {}
    lv_av_dma = 0
    sc_banks = [pb[0], pb[1], pb[5]]

    def av_emit(b, h, jj, mtiles, gjj):
        par = b % 2
        avb = pb[2] if par == 0 else pb[6]
        dnb = pb[3] if par == 0 else pb[7]
        pdst = ab[:, o_p + (gjj % 3) * 512: o_p + (gjj % 3) * 512 + 512]
        p.wait("T", "sv", g_p[gjj])
        if jj == 0 and b >= 2:
            p.wait("T", "ss", lv_ep_s[b - 2])
        p.op("T", lambda t_, h=h, jj=jj, pdst=pdst, avb=avb,
             st_=(jj == 0), sp=(jj == mtiles - 1):
             t_.matmul(avb[:, 0:512],
                       ab[:, o_v + h * 4096 + jj * 128: o_v + h * 4096 + jj * 128 + 128],
                       pdst, start=st_, stop=sp, skip_group_check=True))
        p.op("T", lambda t_, pdst=pdst, dnb=dnb, st_=(jj == 0), sp=(jj == mtiles - 1):
             t_.matmul(dnb[0:1, 0:512], c_1bf[:], pdst,
                       start=st_, stop=sp, skip_group_check=True), inc="sq")
        g_av[gjj] = p.lvl("sq")
        if jj == mtiles - 1:
            g_close[b] = p.lvl("sq")

    def ep_emit(b):
        h, i = b // NBLK, b % NBLK
        par = b % 2
        avb = pb[2] if par == 0 else pb[6]
        dnb = pb[3] if par == 0 else pb[7]
        p.wait("S", "sq", g_close[b])
        if b >= 1:
            p.wait("S", "sv", lv_ep_v[b - 1])
            p.wait("S", "sq", lv_ep_bc[b - 1])
        p.op("S", lambda s, dnb=dnb: s.copy(fa[0:1, f_den:f_den + 512], dnb[0:1, 0:512]))
        p.op("S", lambda s, avb=avb: s.copy(fa[:, f_avf:f_avf + 512], avb[:, 0:512]), inc="ss")
        lv_ep_s[b] = p.lvl("ss")
        p.wait("V", "ss", lv_ep_s[b])
        p.op("V", lambda v: v.reciprocal(fa[0:1, f_den + 512:f_den + 1024],
                                         fa[0:1, f_den:f_den + 512]), inc="sv")
        lv_ep_rec[b] = p.lvl("sv")
        p.wait("T", "sv", lv_ep_rec[b])
        if b >= 1:
            p.wait("T", "sv", lv_ep_v[b - 1])
        p.op("T", lambda t_: t_.matmul(pb[4][:, 0:512], c_1row[:],
                                       fa[0:1, f_den + 512:f_den + 1024],
                                       start=True, stop=True), inc="sq")
        lv_ep_bc[b] = p.lvl("sq")
        p.wait("V", "sq", lv_ep_bc[b])
        if b >= 2:
            p.wait("V", "sy", 16 * (b - 1))
        p.op("V", lambda v, b=b: v.tensor_tensor(
            ab[:, o_avs + (b % 2) * 512: o_avs + (b % 2) * 512 + 512],
            fa[:, f_avf:f_avf + 512], pb[4][:, 0:512], op=ALU.mult), inc="sv")
        lv_ep_v[b] = p.lvl("sv")
        p.wait("Y", "sv", lv_ep_v[b])
        p.op("Y", lambda y, h=h, i=i, b=b: y.dma_start(
            out=avdr[h][:, i * 512:i * 512 + 512],
            in_=ab[:, o_avs + (b % 2) * 512: o_avs + (b % 2) * 512 + 512]),
            inc="sy", n=16)

    gj = 0
    for h in range(4):
        for i in range(NBLK):
            b = h * NBLK + i
            mtiles = 4 * i + 4
            for jj in range(mtiles):
                bs = sc_banks[gj % 3]
                if gj >= 3:
                    p.wait("T", "ss", g_ssq[gj - 3])
                p.op("T", lambda t_, h=h, i=i, jj=jj, bs=bs: t_.matmul(
                    bs[:, 0:512],
                    ab[:, kof(h) + jj * 128: kof(h) + jj * 128 + 128],
                    ab[:, qof(h) + i * 512: qof(h) + i * 512 + 512],
                    start=True, stop=True), inc="sq")
                lv_sc = p.lvl("sq")
                p.wait("S", "sq", lv_sc)
                if gj >= 3:
                    p.wait("S", "sv", g_p[gj - 3])
                p.op("S", lambda s, gj=gj, bs=bs: s.activation(
                    fa[:, f_tsq + (gj % 3) * 512: f_tsq + (gj % 3) * 512 + 512],
                    bs[:, 0:512], AF.Square), inc="ss")
                g_ssq[gj] = p.lvl("ss")
                p.wait("V", "ss", g_ssq[gj])
                if gj >= 3:
                    p.wait("V", "sq", g_av[gj - 3])
                tsqa = fa[:, f_tsq + (gj % 3) * 512: f_tsq + (gj % 3) * 512 + 512]
                pdst = ab[:, o_p + (gj % 3) * 512: o_p + (gj % 3) * 512 + 512]
                if jj >= 4 * i:
                    kd = jj - 4 * i
                    msl = fa[:, f_msk + 384 - 128 * kd: f_msk + 384 - 128 * kd + 512]
                    p.op("V", lambda v, tsqa=tsqa: v.tensor_tensor(tsqa, tsqa, tsqa, op=ALU.mult),
                         inc="gv")
                    p.wait("V", "gv")
                    p.op("V", lambda v, tsqa=tsqa, pdst=pdst, msl=msl: v.tensor_tensor(
                        pdst, tsqa, msl, op=ALU.mult), inc="sv")
                else:
                    p.op("V", lambda v, tsqa=tsqa, pdst=pdst: v.tensor_tensor(
                        pdst, tsqa, tsqa, op=ALU.mult), inc="sv")
                g_p[gj] = p.lvl("sv")
                if jj >= 2:
                    av_emit(b, h, jj - 2, mtiles, gj - 2)
                if jj == 1 and b > 0:
                    ep_emit(b - 1)
                gj += 1
            av_emit(b, h, mtiles - 2, mtiles, gj - 2)
            av_emit(b, h, mtiles - 1, mtiles, gj - 1)
    ep_emit(4 * NBLK - 1)
    lv_av_dma = p.lvl("sy")

    p.wait("G", "sq", lv_sqv[NT - 1])
    p.op("G", lambda g: g.dma_start(out=ab[:, o_wo:o_wo + 8192], in_=woth), inc="sw", n=16)
    lv_wo = p.lvl("sw")
    p.wait("Y", "sy", lv_av_dma)
    for h in range(4):
        p.op("Y", lambda y, h=h: y.dma_start(
            out=ab[:, o_ava + h * 4096: o_ava + h * 4096 + 4096], in_=avdr[h]), inc="sx", n=16)
    lv_ava = p.lvl("sx")
    if DEBUG_TAPS:
        p.op("Y", lambda y: y.dma_start(out=dbg_av, in_=avdr), inc="so", n=16)

    p.wait("T", "sw", lv_wo)
    p.wait("T", "sx", lv_ava)
    p.wait("T", "ss", lv_ep_s[4 * NBLK - 1])
    p.wait("T", "sv", lv_ep_v[4 * NBLK - 1])
    lv_ev = {}
    lv_pyy = {}
    k = 0
    for tt in range(NT):
        for db in range(4):
            bank = pb[k % 4]
            if k >= 4:
                p.wait("T", "sv" if (k - 4) % 2 == 0 else "ss", lv_ev[k - 4])
            for h in range(4):
                p.op("T", lambda t_, tt=tt, db=db, h=h, bank=bank: t_.matmul(
                    bank[:, 0:512],
                    ab[:, o_ava + h * 4096 + tt * 128: o_ava + h * 4096 + tt * 128 + 128],
                    ab[:, o_wo + h * 2048 + db * 512: o_wo + h * 2048 + db * 512 + 512],
                    start=(h == 0), stop=(h == 3)),
                    inc=("sq" if h == 3 else None))
            lv_mm = p.lvl("sq")
            eng = "V" if k % 2 == 0 else "S"
            p.wait(eng, "sq", lv_mm)
            if k >= 2:
                p.wait(eng, "sy", lv_pyy[k - 2])
            dst = fa[:, f_prt + (k % 2) * 512: f_prt + (k % 2) * 512 + 512]
            if eng == "V":
                p.op("V", lambda v, dst=dst, bank=bank: v.tensor_copy(dst, bank[:, 0:512]), inc="sv")
                lv_ev[k] = p.lvl("sv")
            else:
                p.op("S", lambda s, dst=dst, bank=bank: s.copy(dst, bank[:, 0:512]), inc="ss")
                lv_ev[k] = p.lvl("ss")
            p.wait("Y", "sv" if k % 2 == 0 else "ss", lv_ev[k])
            p.op("Y", lambda y, tt=tt, db=db, dst=dst: y.dma_start(
                out=partial[tt][:, db * 512:db * 512 + 512], in_=dst), inc="sy", n=16)
            lv_pyy[k] = p.lvl("sy")
            k += 1
    lv_prt_all = p.lvl("sy")

    p.wait("G", "sy", lv_prt_all)
    p.op("G", lambda g: g.collective_compute(
        "ReduceScatter", ALU.add,
        replica_groups=[[0, 1, 2, 3], [4, 5, 6, 7]],
        ins=[partial.rearrange("t p f -> (t p f)")],
        outs=[rsout]), inc="scc")
    p.op("G", lambda g: g.collective_compute(
        "AllGather", ALU.bypass,
        replica_groups=[[0, 1, 2, 3], [4, 5, 6, 7]],
        ins=[bar_i], outs=[bar_o]), inc="scc")
    p.op("G", lambda g: g.dma_start(out=ab[:, o_wsm:o_wsm + 2432], in_=wsm), inc="sw", n=16)
    p.op("G", lambda g: g.dma_start(out=ab[0:72, o_bd:o_bd + 2048], in_=bdd), inc="sw", n=16)
    p.op("G", lambda g: g.dma_start(out=ab[:, o_adn:o_adn + 3096], in_=adn), inc="sw", n=16)
    lv_p4w = p.lvl("sw")

    rsv = rsout.rearrange("(t p f) -> t p f", t=TT, p=128)
    for e in "TVSY":
        p.wait(e, "scc", 2)
    lv_x1y = {}
    lv_cast = {}
    lv_x1T_s = 0
    for tt in range(TT):
        p.op("Y", lambda y, tt=tt: y.dma_start(out=fa[:, f_rs:f_rs + 2048], in_=rsv[tt]), inc="sx", n=16)
        p.op("Y", lambda y, tt=tt: y.dma_start(out=fa[:, f_xr:f_xr + 2048], in_=xres[tt]), inc="sx", n=16)
        lv_ld = p.lvl("sx")
        p.wait("V", "sx", lv_ld)
        if tt >= 1:
            p.wait("V", "sy", lv_x1y[tt - 1])
            p.wait("V", "ss", lv_cast[tt - 1])
        p.op("V", lambda v: v.tensor_tensor(fa[:, f_x1:f_x1 + 2048], fa[:, f_rs:f_rs + 2048],
                                            fa[:, f_xr:f_xr + 2048], op=ALU.add))
        p.op("V", lambda v: v.reduce_sum(fa[:, f_ln:f_ln + 1], fa[:, f_x1:f_x1 + 2048], axis=AX.X),
             inc="sv")
        lv_add = p.lvl("sv")
        p.wait("S", "sv", lv_add)
        p.op("S", lambda s: s.activation(fa[:, f_rs:f_rs + 2048], fa[:, f_x1:f_x1 + 2048], AF.Square),
             inc="ss")
        lv_sq4 = p.lvl("ss")
        p.wait("V", "ss", lv_sq4)
        p.op("V", lambda v: v.reduce_sum(fa[:, f_ln + 1:f_ln + 2], fa[:, f_rs:f_rs + 2048], axis=AX.X),
             inc="gv")
        p.wait("V", "gv")
        p.op("V", lambda v: v.tensor_scalar(fa[:, f_ln + 2:f_ln + 3], fa[:, f_ln:f_ln + 1],
                                            1.0 / 2048, None, op0=ALU.mult))
        p.op("V", lambda v: v.tensor_scalar(fa[:, f_ln + 3:f_ln + 4], fa[:, f_ln + 1:f_ln + 2],
                                            1.0 / 2048, None, op0=ALU.mult), inc="gv")
        p.wait("V", "gv")
        p.op("V", lambda v: v.tensor_tensor(fa[:, f_ln + 4:f_ln + 5], fa[:, f_ln + 2:f_ln + 3],
                                            fa[:, f_ln + 2:f_ln + 3], op=ALU.mult), inc="gv")
        p.wait("V", "gv")
        p.op("V", lambda v: v.tensor_tensor(fa[:, f_ln + 5:f_ln + 6], fa[:, f_ln + 3:f_ln + 4],
                                            fa[:, f_ln + 4:f_ln + 5], op=ALU.subtract), inc="sv")
        p.wait("S", "sv", p.lvl("sv"))
        p.op("S", lambda s: s.activation(fa[:, f_ln + 6:f_ln + 7], fa[:, f_ln + 5:f_ln + 6],
                                         AF.Sqrt, bias=c_eps[:, 0:1]), inc="ss")
        p.wait("V", "ss", p.lvl("ss"))
        p.op("V", lambda v: v.reciprocal(fa[:, f_ln + 7:f_ln + 8], fa[:, f_ln + 6:f_ln + 7]), inc="gv")
        p.wait("V", "gv")
        p.op("V", lambda v: v.tensor_scalar(fa[:, f_x1:f_x1 + 2048], fa[:, f_x1:f_x1 + 2048],
                                            fa[:, f_ln + 2:f_ln + 3], fa[:, f_ln + 7:f_ln + 8],
                                            op0=ALU.subtract, op1=ALU.mult), inc="sv")
        lv_x1 = p.lvl("sv")
        p.wait("Y", "sv", lv_x1)
        for db4 in range(4):
            p.op("Y", lambda y, tt=tt, db4=db4: y.dma_start(
                out=x1scr[tt][db4],
                in_=fa[:, f_x1 + db4 * 512:f_x1 + db4 * 512 + 512]), inc="sy", n=16)
        lv_x1y[tt] = p.lvl("sy")
        if DEBUG_TAPS:
            p.op("Y", lambda y, tt=tt: y.dma_start(out=dbg_x1[tt], in_=fa[:, f_x1:f_x1 + 2048]),
                 inc="so", n=16)
        p.wait("S", "sv", lv_x1)
        if tt >= 1:
            p.wait("S", "st", p.lvl("st"))
        p.op("S", lambda s: s.copy(ab[:, o_x1stg:o_x1stg + 2048], fa[:, f_x1:f_x1 + 2048]),
             inc="ss")
        lv_cast[tt] = p.lvl("ss")
        p.wait("T", "ss", lv_cast[tt])
        for kt in range(KT):
            bank = pbf[4] if kt % 2 == 0 else pbf[5]
            if kt >= 2:
                p.wait("T", "ss", p.lvl("ss"))
            p.op("T", lambda t_, kt=kt, bank=bank: t_.transpose(
                bank[:, 0:128], ab[:, o_x1stg + kt * 128:o_x1stg + kt * 128 + 128], c_eye[:]),
                inc="st")
            p.wait("S", "st", p.lvl("st"))
            p.op("S", lambda s, kt=kt, tt=tt, bank=bank: s.copy(
                ab[:, o_x1T + kt * 1024 + tt * 128: o_x1T + kt * 1024 + tt * 128 + 128],
                bank[:, 0:128]), inc="ss")
        lv_x1T_s = p.lvl("ss")
        p.wait("Y", "ss", lv_x1T_s)

    p.wait("T", "sw", lv_p4w)
    p.wait("T", "ss", lv_x1T_s)
    lv_xw_last = 0
    lv_rt_s = {}
    for tt in range(TT):
        if tt >= 1:
            p.wait("T", "ss", lv_rt_s[tt - 1])
        for kt in range(KT):
            p.op("T", lambda t_, tt=tt, kt=kt: t_.matmul(
                pb[6][:, 0:152],
                ab[:, o_x1T + kt * 1024 + tt * 128: o_x1T + kt * 1024 + tt * 128 + 128],
                ab[:, o_wsm + kt * 152: o_wsm + kt * 152 + 152],
                start=(kt == 0), stop=(kt == KT - 1)),
                inc=("sq" if kt == KT - 1 else None))
        lv_rm = p.lvl("sq")
        p.wait("S", "sq", lv_rm)
        if tt >= 1:
            p.wait("S", "sv", lv_xw_last)
        p.op("S", lambda s, tt=tt: s.activation(fa[:, f_rt + tt * 8: f_rt + tt * 8 + 8],
                                                pb[6][:, 0:8], AF.Sigmoid))
        p.op("S", lambda s: s.activation(fa[:, f_xa:f_xa + 72], pb[6][:, 8:80],
                                         AF.Copy, scale=c_8th[:, 0:1]))
        p.op("S", lambda s: s.activation(fa[:, f_xa + 72:f_xa + 144], pb[6][:, 80:152],
                                         AF.Copy, scale=c_8th[:, 0:1]), inc="ss")
        lv_rt_s[tt] = p.lvl("ss")
        p.wait("V", "ss", lv_rt_s[tt])
        rt = f_rt + 64
        sc_t = fa[:, f_rt + tt * 8: f_rt + tt * 8 + 8]
        biased = fa[:, rt:rt + 8]
        m1 = fa[:, rt + 8:rt + 9]
        msk = fa[:, rt + 9:rt + 17]
        m2 = fa[:, rt + 17:rt + 18]
        tw = fa[:, rt + 18:rt + 26]
        s2 = fa[:, rt + 26:rt + 27]
        wr = fa[:, rt + 27:rt + 28]
        wgt = fa[:, f_rt + 104 + tt * 8: f_rt + 104 + tt * 8 + 8]
        p.op("V", lambda v, sc_t=sc_t, biased=biased: v.tensor_tensor(biased, sc_t, c_eb[:], op=ALU.add), inc="gv")
        p.wait("V", "gv")
        p.op("V", lambda v, biased=biased, m1=m1: v.reduce_max(m1, biased, axis=AX.X), inc="gv")
        p.wait("V", "gv")
        p.op("V", lambda v, biased=biased, m1=m1, msk=msk: v.tensor_scalar(
            msk, biased, m1[:, 0:1], None, op0=ALU.is_ge), inc="gv")
        p.wait("V", "gv")
        p.op("V", lambda v, biased=biased, msk=msk: v.scalar_tensor_tensor(
            msk, msk, -1e9, biased, op0=ALU.mult, op1=ALU.add), inc="gv")
        p.wait("V", "gv")
        p.op("V", lambda v, msk=msk, m2=m2: v.reduce_max(m2, msk, axis=AX.X), inc="gv")
        p.wait("V", "gv")
        p.op("V", lambda v, biased=biased, m2=m2, tw=tw: v.tensor_scalar(
            tw, biased, m2[:, 0:1], None, op0=ALU.is_ge), inc="gv")
        p.wait("V", "gv")
        p.op("V", lambda v, tw=tw, sc_t=sc_t: v.tensor_tensor(tw, tw, sc_t, op=ALU.mult), inc="gv")
        p.wait("V", "gv")
        p.op("V", lambda v, tw=tw, s2=s2: v.reduce_sum(s2, tw, axis=AX.X), inc="gv")
        p.wait("V", "gv")
        p.op("V", lambda v, s2=s2: v.tensor_scalar(s2, s2, 1e-8, None, op0=ALU.add), inc="gv")
        p.wait("V", "gv")
        p.op("V", lambda v, s2=s2, wr=wr: v.reciprocal(wr, s2), inc="gv")
        p.wait("V", "gv")
        p.op("V", lambda v, tw=tw, wr=wr, wgt=wgt: v.tensor_scalar(
            wgt, tw, wr[:, 0:1], None, op0=ALU.mult), inc="gv")
        p.wait("V", "gv")
        for which, fsrc, oT in ((0, f_xa, o_xgT), (1, f_xa + 72, o_xuT)):
            stg = ab[:, o_xastg + which * 128: o_xastg + which * 128 + 128]
            p.wait("V", "st", p.lvl("st"))
            p.op("V", lambda v, fsrc=fsrc, stg=stg: v.tensor_copy(stg[:, 0:8], fa[:, fsrc:fsrc + 8]))
            for e in range(8):
                p.op("V", lambda v, fsrc=fsrc, stg=stg, e=e, wgt=wgt: v.tensor_scalar(
                    stg[:, 8 + 8 * e:16 + 8 * e], fa[:, fsrc + 8 + 8 * e:fsrc + 16 + 8 * e],
                    wgt[:, e:e + 1], None, op0=ALU.mult),
                    inc=("sv" if e == 7 else None))
            lv_xw = p.lvl("sv")
            p.wait("T", "sv", lv_xw)
            p.wait("T", "sb2", p.lvl("sb2"))
            p.op("T", lambda t_, stg=stg: t_.transpose(pbf[7][0:72, 0:128], stg[:, 0:72], c_eye[:]),
                 inc="st")
            p.wait("S", "st", p.lvl("st"))
            p.op("S", lambda s, oT=oT, tt=tt: s.copy(
                ab[0:72, oT + tt * 128: oT + tt * 128 + 128], pbf[7][0:72, 0:128]), inc="sb2")
            lv_xw_last = lv_xw
    lv_rtall_v = p.lvl("sv")
    lv_rtall_s = p.lvl("sb2")

    lv_g42 = {}
    it = 0
    for blk in range(2):
        for tau in range(FT):
            slot = it % 2
            if it >= 2:
                p.wait("G", "sq", lv_g42[it - 2][0])
            p.op("G", lambda g, tau=tau, slot=slot: g.dma_start(
                out=ab[:, o_wg + slot * 2048: o_wg + slot * 2048 + 2048], in_=wgt_h[tau]),
                inc="swg", n=16)
            p.op("G", lambda g, tau=tau, slot=slot: g.dma_start(
                out=ab[:, o_wu + slot * 2048: o_wu + slot * 2048 + 2048], in_=wup_h[tau]),
                inc="swg", n=16)
            p.op("G", lambda g, tau=tau, slot=slot: g.dma_start(
                out=ab[0:72, o_bgb + slot * 128: o_bgb + slot * 128 + 128],
                in_=bgd[tau]), inc="swg", n=16)
            p.op("G", lambda g, tau=tau, slot=slot: g.dma_start(
                out=ab[0:72, o_bgb + 256 + slot * 128: o_bgb + 256 + slot * 128 + 128],
                in_=bud[tau]), inc="swg", n=16)
            lv_ld = p.lvl("swg")

            bgt, bup = (pb[0], pb[1]) if it % 2 == 0 else (pb[2], pb[3])
            p.wait("T", "swg", lv_ld)
            if it == 0:
                p.wait("T", "sv", lv_rtall_v)
                p.wait("T", "sb2", lv_rtall_s)
                p.wait("T", "ss", lv_rt_s[TT - 1])
            if it >= 2:
                p.wait("T", "sv", lv_g42[it - 2][2])
                p.wait("T", "ss", lv_g42[it - 2][1])
            for kt in range(KT):
                p.op("T", lambda t_, kt=kt, blk=blk, slot=slot, bgt=bgt: t_.matmul(
                    bgt[:, 0:512],
                    ab[:, o_wg + slot * 2048 + kt * 128: o_wg + slot * 2048 + kt * 128 + 128],
                    ab[:, o_x1T + kt * 1024 + blk * 512: o_x1T + kt * 1024 + blk * 512 + 512],
                    start=(kt == 0), stop=False, skip_group_check=True))
            p.op("T", lambda t_, blk=blk, slot=slot, bgt=bgt: t_.matmul(
                bgt[:, 0:512],
                ab[0:72, o_bgb + slot * 128: o_bgb + slot * 128 + 128],
                ab[0:72, o_xgT + blk * 512: o_xgT + blk * 512 + 512],
                start=False, stop=True, skip_group_check=True))
            for kt in range(KT):
                p.op("T", lambda t_, kt=kt, blk=blk, slot=slot, bup=bup: t_.matmul(
                    bup[:, 0:512],
                    ab[:, o_wu + slot * 2048 + kt * 128: o_wu + slot * 2048 + kt * 128 + 128],
                    ab[:, o_x1T + kt * 1024 + blk * 512: o_x1T + kt * 1024 + blk * 512 + 512],
                    start=(kt == 0), stop=False, skip_group_check=True))
            p.op("T", lambda t_, blk=blk, slot=slot, bup=bup: t_.matmul(
                bup[:, 0:512],
                ab[0:72, o_bgb + 256 + slot * 128: o_bgb + 256 + slot * 128 + 128],
                ab[0:72, o_xuT + blk * 512: o_xuT + blk * 512 + 512],
                start=False, stop=True, skip_group_check=True), inc="sq")
            lv_gu = p.lvl("sq")
            p.wait("S", "sq", lv_gu)
            if it >= 2:
                p.wait("S", "sv", lv_g42[it - 2][2])
            p.op("S", lambda s, slot=slot, bgt=bgt: s.activation(
                fa[:, f_tsq + slot * 512: f_tsq + slot * 512 + 512], bgt[:, 0:512], AF.Silu),
                inc="ss")
            lv_sil = p.lvl("ss")
            p.wait("V", "ss", lv_sil)
            p.op("V", lambda v, tau=tau, blk=blk, slot=slot, bup=bup: v.tensor_tensor(
                ab[:, o_hid + tau * 1024 + blk * 512: o_hid + tau * 1024 + blk * 512 + 512],
                fa[:, f_tsq + slot * 512: f_tsq + slot * 512 + 512],
                bup[:, 0:512], op=ALU.mult), inc="sv")
            lv_g42[it] = (lv_gu, lv_sil, p.lvl("sv"))
            it += 1
    lv_hid_all = p.lvl("sv")
    lv_gu_all = p.lvl("sq")
    if DEBUG_TAPS:
        p.wait("Y", "sv", lv_hid_all)
        p.op("Y", lambda y: y.dma_start(out=dbg_hid, in_=ab[:, o_hid:o_hid + FT * 1024]),
             inc="so", n=16)
        p.op("Y", lambda y: y.dma_start(out=dbg_wgt, in_=fa[:, f_rt + 104:f_rt + 104 + 64]),
             inc="so", n=16)

    p.wait("T", "sv", lv_hid_all)
    lv_xd_s = {}
    for tt in range(TT):
        if tt >= 1:
            p.wait("T", "ss", lv_xd_s[tt - 1])
        for tau in range(FT):
            p.op("T", lambda t_, tt=tt, tau=tau: t_.matmul(
                pb[6][:, 0:72],
                ab[:, o_hid + tau * 1024 + tt * 128: o_hid + tau * 1024 + tt * 128 + 128],
                ab[:, o_adn + tau * 72: o_adn + tau * 72 + 72],
                start=(tau == 0), stop=(tau == FT - 1)),
                inc=("sq" if tau == FT - 1 else None))
        lv_xd = p.lvl("sq")
        p.wait("S", "sq", lv_xd)
        if tt >= 1:
            p.wait("S", "sv", p.lvl("sv"))
        p.op("S", lambda s: s.activation(fa[:, f_xa + 144:f_xa + 216], pb[6][:, 0:72],
                                         AF.Copy, scale=c_8th[:, 0:1]), inc="ss")
        lv_xd_s[tt] = p.lvl("ss")
        p.wait("V", "ss", lv_xd_s[tt])
        p.wait("V", "st", p.lvl("st"))
        stg = ab[:, o_xastg:o_xastg + 128]
        p.op("V", lambda v, stg=stg: v.tensor_copy(stg[:, 0:8], fa[:, f_xa + 144:f_xa + 152]))
        for e in range(8):
            p.op("V", lambda v, stg=stg, e=e, tt=tt: v.tensor_scalar(
                stg[:, 8 + 8 * e:16 + 8 * e], fa[:, f_xa + 152 + 8 * e:f_xa + 160 + 8 * e],
                fa[:, f_rt + 104 + tt * 8 + e: f_rt + 104 + tt * 8 + e + 1], None,
                op0=ALU.mult),
                inc=("sv" if e == 7 else None))
        lv_xw = p.lvl("sv")
        p.wait("T", "sv", lv_xw)
        p.wait("T", "sb2", p.lvl("sb2"))
        p.op("T", lambda t_, stg=stg: t_.transpose(pbf[7][0:72, 0:128], stg[:, 0:72], c_eye[:]),
             inc="st")
        p.wait("S", "st", p.lvl("st"))
        p.op("S", lambda s, tt=tt: s.copy(ab[0:72, o_xdT + tt * 128: o_xdT + tt * 128 + 128],
                                          pbf[7][0:72, 0:128]), inc="sb2")
    lv_xdT = p.lvl("sb2")

    p.wait("T", "sb2", lv_xdT)
    p.wait("Y", "sy", lv_x1y[TT - 1])
    p.wait("Y", "sv", p.lvl("sv"))
    lv_d_ld = {}
    lv_d_ev = {}
    lv_d_vv = {}
    lv_d_y = {}
    for db in range(4):
        for tau in range(FT):
            gk = db * FT + tau
            slot = gk % 3
            if gk >= 3:
                p.wait("G", "sq", lv_d_ld[gk - 3][1])
            elif db == 0 and tau == 0:
                p.wait("G", "sq", lv_gu_all)
            p.op("G", lambda g, tau=tau, db=db, slot=slot: g.dma_start(
                out=ab[:, o_wd + slot * 512: o_wd + slot * 512 + 512], in_=wdn_h[db][tau]),
                inc="swd", n=16)
            lv_ld = p.lvl("swd")
            p.wait("T", "swd", lv_ld)
            if tau == 0 and db > 0:
                p.wait("T", "sv", lv_d_ev[db - 1])
            for tt in range(TT):
                p.op("T", lambda t_, tt=tt, tau=tau, db=db, slot=slot: t_.matmul(
                    pb[tt][:, 0:512],
                    ab[:, o_hid + tau * 1024 + tt * 128: o_hid + tau * 1024 + tt * 128 + 128],
                    ab[:, o_wd + slot * 512: o_wd + slot * 512 + 512],
                    start=(tau == 0), stop=False, skip_group_check=True),
                    inc=("sq" if tt == TT - 1 else None))
            lv_d_ld[gk] = (lv_ld, p.lvl("sq"))
        for tt in range(TT):
            kk = db * TT + tt
            p.op("T", lambda t_, tt=tt, db=db: t_.matmul(
                pb[tt][:, 0:512],
                ab[0:72, o_xdT + tt * 128: o_xdT + tt * 128 + 128],
                ab[0:72, o_bd + db * 512: o_bd + db * 512 + 512],
                start=False, stop=True, skip_group_check=True), inc="sq")
            lv_cl = p.lvl("sq")
            slot2 = tt % 2
            if kk >= 2:
                p.wait("Y", "sv", lv_d_vv[kk - 2])
            p.op("Y", lambda y, tt=tt, db=db, slot2=slot2: y.dma_start(
                out=fa[:, f_x1d + slot2 * 512: f_x1d + slot2 * 512 + 512],
                in_=x1scr[tt][db]), inc="sx", n=16)
            lv_x1l = p.lvl("sx")
            p.wait("V", "sq", lv_cl)
            p.wait("V", "sx", lv_x1l)
            if kk >= 2:
                p.wait("V", "sy", lv_d_y[kk - 2])
            p.op("V", lambda v, tt=tt, slot2=slot2: v.tensor_tensor(
                fa[:, f_stg + slot2 * 512: f_stg + slot2 * 512 + 512],
                fa[:, f_x1d + slot2 * 512: f_x1d + slot2 * 512 + 512],
                pb[tt][:, 0:512], op=ALU.add), inc="sv")
            lv_d_vv[kk] = p.lvl("sv")
            p.wait("Y", "sv", lv_d_vv[kk])
            p.op("Y", lambda y, tt=tt, db=db, slot2=slot2: y.dma_start(
                out=dscr[tt][:, db * 512:db * 512 + 512],
                in_=fa[:, f_stg + slot2 * 512: f_stg + slot2 * 512 + 512]), inc="sy", n=16)
            lv_d_y[kk] = p.lvl("sy")
        lv_d_ev[db] = p.lvl("sv")
    lv_dn_all = p.lvl("sy")

    p.wait("Y", "sy", lv_dn_all)
    if DEBUG_TAPS:
        p.op("Y", lambda y: y.dma_start(out=dbg_ds, in_=dscr), inc="so", n=16)
    lv_out_y = {}
    for tt in range(TT):
        p.op("Y", lambda y, tt=tt: y.dma_start(out=fa[:, f_df:f_df + 2048], in_=dscr[tt]),
             inc="sx", n=16)
        lv_ld = p.lvl("sx")
        p.wait("V", "sx", lv_ld)
        p.op("V", lambda v: v.reduce_sum(fa[:, f_ln:f_ln + 1], fa[:, f_df:f_df + 2048], axis=AX.X),
             inc="sv")
        p.wait("S", "sv", p.lvl("sv"))
        p.op("S", lambda s: s.activation(fa[:, f_x1d:f_x1d + 2048], fa[:, f_df:f_df + 2048],
                                         AF.Square), inc="ss")
        p.wait("V", "ss", p.lvl("ss"))
        p.op("V", lambda v: v.reduce_sum(fa[:, f_ln + 1:f_ln + 2], fa[:, f_x1d:f_x1d + 2048],
                                         axis=AX.X), inc="gv")
        p.wait("V", "gv")
        p.op("V", lambda v: v.tensor_scalar(fa[:, f_ln + 2:f_ln + 3], fa[:, f_ln:f_ln + 1],
                                            1.0 / 2048, None, op0=ALU.mult))
        p.op("V", lambda v: v.tensor_scalar(fa[:, f_ln + 3:f_ln + 4], fa[:, f_ln + 1:f_ln + 2],
                                            1.0 / 2048, None, op0=ALU.mult), inc="gv")
        p.wait("V", "gv")
        p.op("V", lambda v: v.tensor_tensor(fa[:, f_ln + 4:f_ln + 5], fa[:, f_ln + 2:f_ln + 3],
                                            fa[:, f_ln + 2:f_ln + 3], op=ALU.mult), inc="gv")
        p.wait("V", "gv")
        p.op("V", lambda v: v.tensor_tensor(fa[:, f_ln + 5:f_ln + 6], fa[:, f_ln + 3:f_ln + 4],
                                            fa[:, f_ln + 4:f_ln + 5], op=ALU.subtract), inc="sv")
        p.wait("S", "sv", p.lvl("sv"))
        p.op("S", lambda s: s.activation(fa[:, f_ln + 6:f_ln + 7], fa[:, f_ln + 5:f_ln + 6],
                                         AF.Sqrt, bias=c_eps[:, 0:1]), inc="ss")
        p.wait("V", "ss", p.lvl("ss"))
        p.op("V", lambda v: v.reciprocal(fa[:, f_ln + 7:f_ln + 8], fa[:, f_ln + 6:f_ln + 7]),
             inc="gv")
        p.wait("V", "gv")
        if tt >= 1:
            p.wait("V", "so", lv_out_y[tt - 1])
        p.op("V", lambda v: v.tensor_scalar(fa[:, f_os:f_os + 2048], fa[:, f_df:f_df + 2048],
                                            fa[:, f_ln + 2:f_ln + 3], fa[:, f_ln + 7:f_ln + 8],
                                            op0=ALU.subtract, op1=ALU.mult), inc="sv")
        lv_o = p.lvl("sv")
        p.wait("Y", "sv", lv_o)
        p.op("Y", lambda y, tt=tt: y.dma_start(out=out[tt], in_=fa[:, f_os:f_os + 2048]),
             inc="so", n=16)
        lv_out_y[tt] = p.lvl("so")

    if DEBUG_TAPS2:
        p.op("Y", lambda y: y.dma_start(
            out=dbg_rs2.rearrange("(t p f) -> t p f", t=TT, p=128), in_=rsv), inc="so", n=16)
        p.op("Y", lambda y: y.dma_start(out=dbg_x12, in_=x1scr), inc="so", n=16)
        p.op("Y", lambda y: y.dma_start(out=dbg_pa2, in_=partial), inc="so", n=16)
        p.op("Y", lambda y: y.dma_start(out=dbg_ds2, in_=dscr), inc="so", n=16)
        p.op("Y", lambda y: y.dma_start(out=dbg_wg2, in_=fa[:, f_rt + 104:f_rt + 104 + 64]),
             inc="so", n=16)
        p.op("Y", lambda y: y.dma_start(out=dbg_h2, in_=ab[:, o_hid:o_hid + FT * 1024]),
             inc="so", n=16)
        p.op("Y", lambda y: y.dma_start(out=dbg_xt2, in_=ab[:, o_x1T:o_x1T + KT * 1024]),
             inc="so", n=16)
        p.op("Y", lambda y: y.dma_start(out=dbg_xa2, in_=ab[:, o_xgT:o_xgT + 3072]),
             inc="so", n=16)
    p.wait("Y", "so", p.lvl("so"))

    with nc.Block() as block:
        @block.tensor
        def _(e):
            for fn in p.q["T"]:
                fn(e)

        @block.vector
        def _(e):
            for fn in p.q["V"]:
                fn(e)

        @block.scalar
        def _(e):
            for fn in p.q["S"]:
                fn(e)

        @block.gpsimd
        def _(e):
            for fn in p.q["G"]:
                fn(e)

        @block.sync
        def _(e):
            for fn in p.q["Y"]:
                fn(e)

    ctx.close()
    return nc


def _c(a):
    return np.ascontiguousarray(a, dtype=np.float32)


def prep_inputs(inp):
    x = np.asarray(inp["L0_attn_x"], np.float32)
    wqkv = np.asarray(inp["L0_attn_w_qkv"], np.float32)
    wout = np.asarray(inp["L0_attn_w_out"], np.float32)
    lnsc = np.asarray(inp["L0_attn_ln_scale"], np.float32)
    lnbi = np.asarray(inp["L0_attn_ln_bias"], np.float32)
    Wr = np.asarray(inp["L0_ffn_W_router"], np.float32)
    Wg = np.asarray(inp["L0_ffn_W_gate"], np.float32)
    Wu = np.asarray(inp["L0_ffn_W_up"], np.float32)
    Wd = np.asarray(inp["L0_ffn_W_down"], np.float32)
    sAg = np.asarray(inp["L0_ffn_sA_gate"], np.float32)
    sBg = np.asarray(inp["L0_ffn_sB_gate"], np.float32)
    sAu = np.asarray(inp["L0_ffn_sA_up"], np.float32)
    sBu = np.asarray(inp["L0_ffn_sB_up"], np.float32)
    sAd = np.asarray(inp["L0_ffn_sA_down"], np.float32)
    sBd = np.asarray(inp["L0_ffn_sB_down"], np.float32)
    Ag = np.asarray(inp["L0_ffn_A_gate"], np.float32)
    Bg = np.asarray(inp["L0_ffn_B_gate"], np.float32)
    Au = np.asarray(inp["L0_ffn_A_up"], np.float32)
    Bu = np.asarray(inp["L0_ffn_B_up"], np.float32)
    Ad = np.asarray(inp["L0_ffn_A_down"], np.float32)
    Bd = np.asarray(inp["L0_ffn_B_down"], np.float32)
    eb = np.asarray(inp["L0_ffn_expert_bias"], np.float32)

    pos = np.arange(4096, dtype=np.float64)
    invf = 1.0 / (10000.0 ** (np.arange(0, 128, 2, dtype=np.float64) / 128.0))
    ang = pos[:, None] * invf[None, :]
    cos = np.cos(ang).astype(np.float32)
    sin = np.sin(ang).astype(np.float32)
    ropeC = _c(cos.reshape(32, 128, 64).transpose(1, 0, 2).reshape(128, 2048))
    ropeS = _c(sin.reshape(32, 128, 64).transpose(1, 0, 2).reshape(128, 2048))
    lncol = _c(np.stack([lnsc, lnbi], axis=1))
    ebrow = _c(eb.reshape(1, 8))
    maskst = _c((np.arange(128)[:, None] <= np.arange(896)[None, :] - 384).astype(np.float32))
    eyebf = np.eye(128, dtype=np.float32).astype(BF)

    Agc = np.concatenate([sAg] + [Ag[e] for e in range(8)], axis=1)
    Auc = np.concatenate([sAu] + [Au[e] for e in range(8)], axis=1)
    Adc = np.concatenate([sAd] + [Ad[e] for e in range(8)], axis=1)
    Bgc = np.concatenate([sBg, Bg.reshape(64, DFF)], axis=0)
    Buc = np.concatenate([sBu, Bu.reshape(64, DFF)], axis=0)
    Bdc = np.concatenate([sBd, Bd.reshape(64, 2048)], axis=0)
    wsm_full = np.concatenate([Wr, Agc, Auc], axis=1)
    wsm = _c(wsm_full.reshape(16, 128, 152).transpose(1, 0, 2).reshape(128, 16 * 152))
    wgt_h = _c(Wg.reshape(16, 128, 43, 128).transpose(2, 1, 0, 3).reshape(43, 128, 2048))
    wup_h = _c(Wu.reshape(16, 128, 43, 128).transpose(2, 1, 0, 3).reshape(43, 128, 2048))
    wdn_h = _c(Wd.reshape(43, 128, 4, 512).transpose(2, 0, 1, 3))
    adn = _c(Adc.reshape(43, 128, 72).transpose(1, 0, 2).reshape(128, 43 * 72))
    bg_ = _c(Bgc.reshape(72, 43, 128).transpose(1, 0, 2))
    bu_ = _c(Buc.reshape(72, 43, 128).transpose(1, 0, 2))
    bd_ = _c(Bdc)

    maps = []
    for c in range(8):
        g, loc = c // 4, c % 4
        xg = x[g]
        xTh = _c(xg.reshape(32, 128, 16, 128).transpose(0, 3, 2, 1).reshape(32, 128, 2048))
        wq = wqkv[:, 512 * loc:512 * loc + 512]
        wk = wqkv[:, 2048 + 512 * loc:2048 + 512 * loc + 512]
        wv_ = wqkv[:, 4096 + 512 * loc:4096 + 512 * loc + 512]
        wqk_i = np.empty((2048, 1024), np.float32)
        wqk_i[:, 0:512] = wq
        wqk_i[:, 512:1024] = wk
        wqk_h = _c(wqk_i.reshape(16, 128, 1024).transpose(1, 0, 2).reshape(128, 16384))
        wv_h = _c(wv_.reshape(16, 128, 512).transpose(1, 0, 2).reshape(128, 8192))
        xres_ = _c(xg[1024 * loc:1024 * loc + 1024].reshape(8, 128, 2048))
        woth_ = _c(wout[512 * loc:512 * loc + 512, :].reshape(4, 128, 2048)
                   .transpose(1, 0, 2).reshape(128, 8192))
        maps.append(dict(
            xTh=xTh, wqk=wqk_h, wv=wv_h, ropeC=ropeC, ropeS=ropeS, lncol=lncol,
            ebrow=ebrow, maskst=maskst, eyebf=eyebf, xres=xres_, woth=woth_,
            wsm=wsm, wgt_h=wgt_h, wup_h=wup_h, wdn_h=wdn_h, bgd=bg_, bud=bu_,
            bdd=bd_, adn=adn))
    return maps


def kernel(**inputs):
    maps = prep_inputs(inputs)
    nc = build_program()
    res = run_bass_kernel_spmd(nc, maps, core_ids=list(range(8)))
    out = np.empty((2, 4096, 2048), np.float32)
    for c in range(8):
        g, loc = c // 4, c % 4
        o = np.asarray(res.results[c]["out"], np.float32).reshape(1024, 2048)
        out[g, 1024 * loc:1024 * loc + 1024, :] = o
    return out
